# version 07
# baseline (speedup 1.0000x reference)
import numpy as np
from contextlib import ExitStack

import concourse.bass as bass
import concourse.mybir as mybir
from concourse.bass_utils import run_bass_kernel_spmd

F32 = mybir.dt.float32
BF16 = mybir.dt.bfloat16
U8 = mybir.dt.uint8
AF = mybir.ActivationFunctionType
ALU = mybir.AluOpType
AX = mybir.AxisListType

B, S, D = 4, 4096, 4096
NQ, NKV, DK, GRP = 32, 4, 128, 8
QTW = 512
OWN = 2048
NMT = 16
HID, BOT, RDIM, NSLOT, VB = 4096, 64, 48, 1024, 256
EPS = 1e-5
ASCALE = 1.0 / float(np.sqrt(128.0))
RSCALE = 1.0 / float(np.sqrt(48.0))
FSCALE = 1.0 / 8.0
QTILES = ([0, 3, 4, 7], [1, 2, 5, 6])

_DBG = set()
_SIM_SINGLE = False

CEPS, C16K, C1M6, CHALF, C2P5, C5, C1P5, CEXIT, C0, C1, CM1, CNEG = range(12)
CST_VALS = {CEPS: EPS, C16K: 1.0 / 16384.0, C1M6: 1e-6, CHALF: 0.5, C2P5: 2.5,
            C5: 5.0, C1P5: 1.0 / 1.5, CEXIT: 0.05, C0: 0.0, C1: 1.0,
            CM1: -1.0, CNEG: -30000.0}


class SB:
    def __init__(self, h):
        self.h = h
        self.n = 0

    def inc(self, k=1):
        self.n += k
        return self.n


class Builder:
    def __init__(self):
        self.nc = bass.Bass("TRN2", target_bir_lowering=False)
        self.es = ExitStack()
        self._semnames = set()

    def sems(self, names):
        if not hasattr(self, "_sempool"):
            self._sempool = {}
        out = {}
        for n in names:
            if n not in self._sempool:
                self._sempool[n] = SB(self.es.enter_context(self.nc.semaphore(n)))
            out[n] = self._sempool[n]
        return out

    def sbuf(self, stack, name, shape, dt):
        return stack.enter_context(self.nc.sbuf_tensor(name, shape, dt))

    def psum(self, stack, name, shape, dt=F32):
        return stack.enter_context(self.nc.psum_tensor(name, shape, dt))

    def declare_io(self):
        nc = self.nc

        def din(name, shape, dt=F32):
            return nc.dram_tensor(name, shape, dt, kind="ExternalInput").ap()

        def dscr(name, shape, dt):
            kind = "ExternalOutput" if name in _DBG else "Internal"
            return nc.dram_tensor(name, shape, dt, kind=kind).ap()

        self.x_own = din("x_own", [OWN, D])
        self.x_oth = din("x_oth", [OWN, D])
        self.Wq = din("Wq", [16, 32, 128, 256])
        self.Wk = din("Wk", [128, 32 * 512])
        self.Wv = din("Wv", [128, 32 * 512])
        self.Wo = din("Wo", [4, 128, 32 * 1024])
        self.WqFT = din("WqFT", [128, 32 * BOT])
        self.WrT = din("WrT", [BOT, RDIM])
        self.auxkT = din("auxkT", [BOT, NSLOT])
        self.auxv = din("auxv", [128, 8 * HID])
        self.WvdT = din("WvdT", [128, 32 * VB])
        self.WvuT = din("WvuT", [128, 2 * HID])
        self.Wu1T = din("Wu1T", [HID, 1024])
        self.Wu2T = din("Wu2T", [128, 8])
        self.bu1c = din("bu1c", [128, 8])
        self.bu2 = din("bu2", [1, 1])
        self.gw1 = din("gw1", [1, 1])
        self.gbias = din("gbias", [1, 1])
        self.logrel = din("logrel", [1, NSLOT])
        self.ident = din("ident", [128, 128])
        self.maskp = din("maskp", [128, 2048])

        self.out_own = nc.dram_tensor("out_own", [OWN, D], F32,
                                      kind="ExternalOutput").ap()

        self.s_ln1T = dscr("s_ln1T", [32, 128, S], BF16)
        self.s_q = dscr("s_q", [4, 128, NQ * QTW], BF16)
        self.s_ot = dscr("s_ot", [NMT, 128, NQ * 128], BF16)
        self.s_x2 = dscr("s_x2", [OWN, D], F32)
        self.s_ln2T = dscr("s_ln2T", [4, 128, 32 * 512], BF16)
        self.s_z2 = dscr("s_z2", [OWN, D], BF16)
        self.s_lr = dscr("s_lr", [1, OWN], F32)
        self.s_row = dscr("s_row", [4, QTW], F32)
        self.s_row32 = dscr("s_row32", [32, QTW], F32)
        self.s_cc_in = dscr("s_cc_in", [1, 4], F32)
        self.s_cc_out = dscr("s_cc_out", [1, 4], F32)
        self.s_kv = dscr("s_kv", [1, 4], F32)

    def persistent_sbuf(self):
        es = self.es
        self.onescol = self.sbuf(es, "onescol", [128, 1], BF16)
        self.onescolf = self.sbuf(es, "onescolf", [128, 1], F32)
        self.id_bf = self.sbuf(es, "id_bf", [128, 128], BF16)
        self.maskp_bf = self.sbuf(es, "maskp_bf", [128, 2048], BF16)
        self.cst = self.sbuf(es, "cst", [128, 12], F32)
        self.negc = self.sbuf(es, "negc", [128, NSLOT], F32)
        self.logrelbc = self.sbuf(es, "logrelbc", [128, NSLOT], F32)
        self.WqFT_sb = self.sbuf(es, "WqFT_sb", [128, 32 * BOT], BF16)
        self.WrT_sb = self.sbuf(es, "WrT_sb", [BOT, RDIM], BF16)
        self.auxkT_sb = self.sbuf(es, "auxkT_sb", [BOT, NSLOT], BF16)
        self.rkT_sb = self.sbuf(es, "rkT_sb", [RDIM, NSLOT], BF16)
        self.Wu2T_sb = self.sbuf(es, "Wu2T_sb", [128, 8], BF16)
        self.bu1_sb = self.sbuf(es, "bu1_sb", [128, 8], F32)
        self.bu2c = self.sbuf(es, "bu2c", [128, 1], F32)
        self.gw1c = self.sbuf(es, "gw1c", [128, 1], F32)
        self.gbc = self.sbuf(es, "gbc", [128, 1], F32)
        self.Vsum_bf = self.sbuf(es, "Vsum_bf", [128, NKV], BF16)
        self.mean2_sb = self.sbuf(es, "mean2_sb", [128, NMT], F32)
        self.rstd2_sb = self.sbuf(es, "rstd2_sb", [128, NMT], F32)
        self.logv_sb = self.sbuf(es, "logv_sb", [128, NMT], F32)
        self.nvh_sb = self.sbuf(es, "nvh_sb", [128, NMT], F32)

    def p0_consts(self):
        nc = self.nc
        sm = self.sems(["p0g", "p0y", "p0c"])
        with nc.Block() as blk:
            @blk.vector
            def _(v):
                v.memset(self.onescol[:, :], 1.0)
                v.memset(self.onescolf[:, :], 1.0)
                for k, val in CST_VALS.items():
                    v.memset(self.cst[:, k:k + 1], val)
                v.memset(self.negc[:, :], -30000.0)

            @blk.gpsimd
            def _(g):
                g.dma_start(self.id_bf[:, :], self.ident).then_inc(sm["p0g"].h, 16)
                g.dma_start(self.maskp_bf[:, :], self.maskp).then_inc(sm["p0g"].h, 16)
                g.dma_start(self.WqFT_sb[:, :], self.WqFT
                            ).then_inc(sm["p0g"].h, 16)
                g.dma_start(self.WrT_sb[:, :], self.WrT).then_inc(sm["p0g"].h, 16)
                g.dma_start(self.auxkT_sb[:, :], self.auxkT).then_inc(sm["p0g"].h, 16)
                g.dma_start(self.Wu2T_sb[:, :], self.Wu2T
                            ).then_inc(sm["p0g"].h, 16)
                g.wait_ge(sm["p0g"].h, sm["p0g"].inc(6 * 16))

            @blk.sync
            def _(sy):
                sy.dma_start(self.logrelbc[:, :],
                             self.logrel.to_broadcast([128, NSLOT])
                             ).then_inc(sm["p0y"].h, 16)
                sy.dma_start(self.bu1_sb[:, :], self.bu1c).then_inc(sm["p0y"].h, 16)
                sy.dma_start(self.bu2c[:, :], self.bu2.to_broadcast([128, 1])
                             ).then_inc(sm["p0y"].h, 16)
                sy.dma_start(self.gw1c[:, :], self.gw1.to_broadcast([128, 1])
                             ).then_inc(sm["p0y"].h, 16)
                sy.dma_start(self.gbc[:, :], self.gbias.to_broadcast([128, 1])
                             ).then_inc(sm["p0y"].h, 16)
                sy.wait_ge(sm["p0y"].h, sm["p0y"].inc(5 * 16))

        with ExitStack() as ps:
            ps_rk = self.psum(ps, "ps_rk", [RDIM, NSLOT])
            with nc.Block() as blk:
                @blk.tensor
                def _(t):
                    for i in range(2):
                        t.matmul(ps_rk[:, i * 512:(i + 1) * 512], self.WrT_sb[:, :],
                                 self.auxkT_sb[:, i * 512:(i + 1) * 512],
                                 start=True, stop=True).then_inc(sm["p0c"].h, 1)

                @blk.vector
                def _(v):
                    v.wait_ge(sm["p0c"].h, sm["p0c"].inc(2))
                    v.tensor_copy(self.rkT_sb[:, :], ps_rk[:, :])

    def _ln_transpose(self, tag, ntiles, src_of, lnT_scr, tok_base,
                      save_stats=None, qtile_blocked=False):
        nc = self.nc
        names = ["x0", "x1", "x2", "x3", "st", "sq", "ap", "tr", "ev",
                 "dm0", "dm1"]
        smr = self.sems(["lt" + n for n in names])
        sm = {tag + n: smr["lt" + n] for n in names}
        with ExitStack() as ph:
            xb = self.sbuf(ph, tag + "xb", [128, 4 * D], F32)
            lnb = self.sbuf(ph, tag + "lnb", [128, 8 * D], BF16)
            stt = self.sbuf(ph, tag + "stt", [128, 48], F32)
            agg = self.sbuf(ph, tag + "agg", [128, 8], F32)
            rstd = self.sbuf(ph, tag + "rstd", [128, 4], F32)
            nmr = self.sbuf(ph, tag + "nmr", [128, 4], F32)
            stg = self.sbuf(ph, tag + "stg", [128, 2 * 1024], BF16)
            pst = self.psum(ph, tag + "pst", [128, 2 * 1024], BF16)

            ops = {k: [] for k in "tsvyg"}
            x_done, ap_done, sq_done = {}, {}, {}
            tr_done, ev_done, dm_done = {}, {}, {}
            nsup = ntiles // 4

            for T in range(nsup):
                sp = T % 2
                for sub in range(4):
                    tt = T * 4 + sub
                    xs = tt % 4
                    src = src_of(tt)

                    def ld(sy, src=src, xs=xs, tt=tt):
                        if tt >= 4:
                            sy.wait_ge(sm[tag + "ap"].h, ap_done[tt - 4])
                        sy.dma_start(xb[:, xs * D:(xs + 1) * D], src
                                     ).then_inc(sm[tag + "x%d" % xs].h, 16)
                    ops["y"].append(ld)
                    x_done[tt] = sm[tag + "x%d" % (tt % 4)].inc(16)

                    sl4 = tt % 4

                    def stats(v, xs=xs, tt=tt, T=T, sl4=sl4):
                        v.wait_ge(sm[tag + "x%d" % xs].h, x_done[tt])
                        for ch in range(8):
                            v.bn_stats(stt[:, ch * 6:(ch + 1) * 6],
                                       xb[:, xs * D + ch * 512:
                                          xs * D + (ch + 1) * 512])
                        v.drain()
                        v.bn_aggr(agg[:, sl4 * 2:sl4 * 2 + 2], stt[:, :])
                        v.drain()
                        if save_stats is not None:
                            save_stats(v, tt, agg[:, sl4 * 2:sl4 * 2 + 2])
                        if tt >= 4:
                            v.wait_ge(sm[tag + "ap"].h, ap_done[tt - 4])
                        v.tensor_scalar(rstd[:, sl4:sl4 + 1],
                                        agg[:, sl4 * 2 + 1:sl4 * 2 + 2],
                                        self.cst[:, CEPS:CEPS + 1], None,
                                        op0=ALU.add).then_inc(sm[tag + "st"].h, 1)
                    ops["v"].append(stats)
                    st_v = sm[tag + "st"].inc()

                    def sqr(s, st_v=st_v, sl4=sl4):
                        s.wait_ge(sm[tag + "st"].h, st_v)
                        s.activation(rstd[:, sl4:sl4 + 1], rstd[:, sl4:sl4 + 1],
                                     AF.Sqrt).then_inc(sm[tag + "sq"].h, 1)
                    ops["s"].append(sqr)
                    sq_done[tt] = sm[tag + "sq"].inc()

                    def rcp(v, tt=tt, sl4=sl4):
                        v.wait_ge(sm[tag + "sq"].h, sq_done[tt])
                        v.reciprocal(rstd[:, sl4:sl4 + 1], rstd[:, sl4:sl4 + 1])
                        v.drain()
                        v.tensor_scalar(nmr[:, sl4:sl4 + 1],
                                        agg[:, sl4 * 2:sl4 * 2 + 1],
                                        rstd[:, sl4:sl4 + 1],
                                        self.cst[:, CM1:CM1 + 1],
                                        op0=ALU.mult, op1=ALU.mult
                                        ).then_inc(sm[tag + "st"].h, 1)
                    ops["v"].append(rcp)
                    rc_v = sm[tag + "st"].inc()

                    def apply(s, sp=sp, sub=sub, xs=xs, tt=tt, T=T, rc_v=rc_v,
                              sl4=sl4):
                        s.wait_ge(sm[tag + "st"].h, rc_v)
                        if T >= 2 and sub == 0:
                            s.wait_ge(sm[tag + "tr"].h, tr_done[(T - 2, 15)])
                        s.activation(
                            lnb[:, (sp * 4 + sub) * D:(sp * 4 + sub + 1) * D],
                            xb[:, xs * D:(xs + 1) * D], AF.Identity,
                            scale=rstd[:, sl4:sl4 + 1], bias=nmr[:, sl4:sl4 + 1]
                        ).then_inc(sm[tag + "ap"].h, 1)
                    ops["s"].append(apply)
                    ap_done[tt] = sm[tag + "ap"].inc()

                for dp in range(16):
                    def tr(t, T=T, sp=sp, dp=dp):
                        t.wait_ge(sm[tag + "ap"].h, ap_done[T * 4 + 3])
                        if T > 0 or dp >= 2:
                            pT, pdp = (T, dp - 2) if dp >= 2 else (T - 1, 14 + dp)
                            t.wait_ge(sm[tag + "ev"].h, ev_done[(pT, pdp)])
                        for j in range(2):
                            dc = dp * 2 + j
                            for sub in range(4):
                                inst = t.transpose(
                                    pst[:, (dp % 2) * 1024 + j * 512 + sub * 128:
                                        (dp % 2) * 1024 + j * 512 + (sub + 1) * 128],
                                    lnb[:, (sp * 4 + sub) * D + dc * 128:
                                        (sp * 4 + sub) * D + (dc + 1) * 128],
                                    self.id_bf[:, :])
                        inst.then_inc(sm[tag + "tr"].h, 1)
                    ops["t"].append(tr)
                    tr_done[(T, dp)] = sm[tag + "tr"].inc()

                    def ev(s, T=T, dp=dp):
                        s.wait_ge(sm[tag + "tr"].h, tr_done[(T, dp)])
                        if T > 0 or dp >= 2:
                            pT, pdp = (T, dp - 2) if dp >= 2 else (T - 1, 14 + dp)
                            s.wait_ge(sm[tag + "dm%d" % (pdp % 2)].h,
                                      dm_done[(pT, pdp)])
                        s.activation(stg[:, (dp % 2) * 1024:(dp % 2 + 1) * 1024],
                                     pst[:, (dp % 2) * 1024:(dp % 2 + 1) * 1024],
                                     AF.Copy).then_inc(sm[tag + "ev"].h, 1)
                    ops["s"].append(ev)
                    ev_done[(T, dp)] = sm[tag + "ev"].inc()

                    def st(sy, T=T, dp=dp):
                        sy.wait_ge(sm[tag + "ev"].h, ev_done[(T, dp)])
                        if qtile_blocked:
                            dst = lnT_scr[T][:, dp * 1024:(dp + 1) * 1024]
                        else:
                            dst = lnT_scr[dp * 2:dp * 2 + 2, :,
                                          tok_base + T * 512:
                                          tok_base + (T + 1) * 512
                                          ].transpose([1, 0, 2])
                        sy.dma_start(
                            dst if qtile_blocked else dst,
                            stg[:, (dp % 2) * 1024:(dp % 2 + 1) * 1024].rearrange(
                                "p (a f) -> p a f", a=2)
                            if not qtile_blocked else
                            stg[:, (dp % 2) * 1024:(dp % 2 + 1) * 1024]
                        ).then_inc(sm[tag + "dm%d" % (dp % 2)].h, 16)
                    ops["y"].append(st)
                    dm_done[(T, dp)] = sm[tag + "dm%d" % (dp % 2)].inc(16)

            with nc.Block() as blk:
                blk.sync(lambda sy: [f(sy) for f in ops["y"]])
                blk.vector(lambda v: [f(v) for f in ops["v"]])
                blk.scalar(lambda s: [f(s) for f in ops["s"]])
                blk.tensor(lambda t: [f(t) for f in ops["t"]])

            with nc.Block() as blk:
                f0, f1 = sm[tag + "dm0"].n, sm[tag + "dm1"].n

                @blk.sync
                def _(sy):
                    sy.wait_ge(sm[tag + "dm0"].h, f0)
                    sy.wait_ge(sm[tag + "dm1"].h, f1)

    def p1a_ln1(self):
        def src_of(tt):
            src = self.x_own if tt < 16 else self.x_oth
            r0 = (tt % 16) * 128
            return src[r0:r0 + 128, :]
        self._ln_transpose("l1", 32, src_of, self.s_ln1T, 0)

    def p2_q(self):
        nc = self.nc
        sm = self.sems(["q2g0", "q2g1", "q2g2", "q2g3", "q2ld", "q2mm", "q2ev", "q2dm0", "q2dm1"])
        with ExitStack() as ph:
            qln = self.sbuf(ph, "qln", [128, 32 * OWN], BF16)
            wqt = self.sbuf(ph, "wqt", [128, 4 * 256], BF16)
            qstg = self.sbuf(ph, "qstg", [128, 2 * 4096], BF16)
            psq = self.psum(ph, "psq", [128, 4096], F32)

            with nc.Block() as blk:
                @blk.sync
                def _(sy):
                    for i in range(4):
                        sy.dma_start(
                            qln[:, i * 8 * OWN:(i + 1) * 8 * OWN].rearrange(
                                "p (a f) -> p a f", a=8),
                            self.s_ln1T[i * 8:(i + 1) * 8, :, 0:OWN]
                            .transpose([1, 0, 2])
                        ).then_inc(sm["q2ld"].h, 16)
                    sy.wait_ge(sm["q2ld"].h, sm["q2ld"].inc(64))

            ops = {k: [] for k in "tsvyg"}
            wq_done, mm_done, ev_done, dm_done = {}, {}, {}, {}
            for fg in range(16):
                for dc in range(32):
                    i = fg * 32 + dc
                    ws = i % 4

                    def wld(g, fg=fg, dc=dc, ws=ws, i=i):
                        if i >= 4:
                            g.wait_ge(sm["q2mm"].h, mm_done[i - 4])
                        g.dma_start(wqt[:, ws * 256:(ws + 1) * 256],
                                    self.Wq[fg, dc]
                                    ).then_inc(sm["q2g%d" % ws].h, 16)
                    ops["g"].append(wld)
                    wq_done[i] = sm["q2g%d" % ws].inc(16)

                    def mm(t, fg=fg, dc=dc, ws=ws, i=i):
                        t.wait_ge(sm["q2g%d" % ws].h, wq_done[i])
                        if dc == 0 and fg > 0:
                            t.wait_ge(sm["q2ev"].h, ev_done[fg - 1])
                        first, last = dc == 0, dc == 31
                        for f2 in range(2):
                            for qt in range(4):
                                inst = t.matmul(
                                    psq[:, (f2 * 4 + qt) * 512:
                                        (f2 * 4 + qt + 1) * 512],
                                    wqt[:, ws * 256 + f2 * 128:
                                        ws * 256 + (f2 + 1) * 128],
                                    qln[:, dc * OWN + qt * 512:
                                        dc * OWN + (qt + 1) * 512],
                                    start=first, stop=last)
                        inst.then_inc(sm["q2mm"].h, 1)
                    ops["t"].append(mm)
                    mm_done[i] = sm["q2mm"].inc()

                def qev(v, fg=fg):
                    v.wait_ge(sm["q2mm"].h, mm_done[fg * 32 + 31])
                    if fg >= 2:
                        v.wait_ge(sm["q2dm%d" % (fg % 2)].h, dm_done[fg - 2])
                    inst = v.tensor_copy(
                        qstg[:, (fg % 2) * 4096:(fg % 2 + 1) * 4096], psq[:, :])
                    inst.then_inc(sm["q2ev"].h, 1)
                ops["v"].append(qev)
                ev_done[fg] = sm["q2ev"].inc()

                def qdm(sy, fg=fg):
                    sy.wait_ge(sm["q2ev"].h, ev_done[fg])
                    for qt in range(4):
                        sy.dma_start(
                            self.s_q[qt][:, fg * 2 * QTW:(fg * 2 + 2) * QTW]
                            .rearrange("p (a f) -> p a f", a=2),
                            qstg[:, (fg % 2) * 4096:(fg % 2 + 1) * 4096]
                            .rearrange("p (a f) -> p a f", a=2)[
                                :, :, qt * QTW:(qt + 1) * QTW]
                        ).then_inc(sm["q2dm%d" % (fg % 2)].h, 16)
                ops["y"].append(qdm)
                dm_done[fg] = sm["q2dm%d" % (fg % 2)].inc(64)

            with nc.Block() as blk:
                blk.gpsimd(lambda g: [f(g) for f in ops["g"]])
                blk.tensor(lambda t: [f(t) for f in ops["t"]])
                blk.vector(lambda v: [f(v) for f in ops["v"]])
                blk.sync(lambda sy: [f(sy) for f in ops["y"]])

            with nc.Block() as blk:
                f0, f1 = sm["q2dm0"].n, sm["q2dm1"].n

                @blk.sync
                def _(sy):
                    sy.wait_ge(sm["q2dm0"].h, f0)
                    sy.wait_ge(sm["q2dm1"].h, f1)

    def p1b_kv(self, kv_stack):
        nc = self.nc
        self.KT_sb = self.sbuf(kv_stack, "KT_sb", [128, NKV * S], BF16)
        self.V_sb = self.sbuf(kv_stack, "V_sb", [128, 32 * 512], BF16)
        sm = self.sems(["kvw", "kvl0", "kvl1", "kvm", "kvev", "vsm", "vse"])
        with ExitStack() as ph:
            Wk_sb = self.sbuf(ph, "Wk_sb", [128, 32 * 512], BF16)
            Wv_sb = self.sbuf(ph, "Wv_sb", [128, 32 * 512], BF16)
            lnt = self.sbuf(ph, "kvlnt", [128, 2 * 1024], BF16)
            psk = self.psum(ph, "psk", [128, 4096], F32)

            with nc.Block() as blk:
                @blk.gpsimd
                def _(g):
                    g.dma_start(Wk_sb[:, :], self.Wk).then_inc(sm["kvw"].h, 16)
                    g.dma_start(Wv_sb[:, :], self.Wv).then_inc(sm["kvw"].h, 16)
                    g.wait_ge(sm["kvw"].h, sm["kvw"].inc(2 * 16))

            ops = {k: [] for k in "tsvyg"}
            ld_done, mm_done, ev_done = {}, {}, {}
            for kind in ("K", "V"):
                for p in range(4):
                    for dc in range(32):
                        i = (0 if kind == "K" else 128) + p * 32 + dc
                        ws = i % 2

                        def lld(sy, p=p, dc=dc, ws=ws, i=i):
                            if i >= 2:
                                sy.wait_ge(sm["kvm"].h, mm_done[i - 2])
                            sy.dma_start(
                                lnt[:, ws * 1024:(ws + 1) * 1024],
                                self.s_ln1T[dc, :, p * 1024:(p + 1) * 1024]
                            ).then_inc(sm["kvl%d" % ws].h, 16)
                        ops["y"].append(lld)
                        ld_done[i] = sm["kvl%d" % ws].inc(16)

                        def kmm(t, kind=kind, p=p, dc=dc, ws=ws, i=i):
                            t.wait_ge(sm["kvl%d" % ws].h, ld_done[i])
                            j = i // 32
                            if dc == 0 and j >= 1:
                                t.wait_ge(sm["kvev"].h, ev_done[j - 1])
                            first, last = dc == 0, dc == 31
                            if kind == "K":
                                for g4 in range(4):
                                    for t2 in range(2):
                                        inst = t.matmul(
                                            psq_slice(psk, g4 * 2 + t2),
                                            Wk_sb[:, dc * 512 + g4 * 128:
                                                  dc * 512 + (g4 + 1) * 128],
                                            lnt[:, ws * 1024 + t2 * 512:
                                                ws * 1024 + (t2 + 1) * 512],
                                            start=first, stop=last)
                            else:
                                for t8 in range(8):
                                    inst = t.matmul(
                                        psq_slice(psk, t8),
                                        lnt[:, ws * 1024 + t8 * 128:
                                            ws * 1024 + (t8 + 1) * 128],
                                        Wv_sb[:, dc * 512:(dc + 1) * 512],
                                        start=first, stop=last)
                            inst.then_inc(sm["kvm"].h, 1)
                        ops["t"].append(kmm)
                        mm_done[i] = sm["kvm"].inc()

                    def kev(v, kind=kind, p=p):
                        i = (0 if kind == "K" else 128) + p * 32 + 31
                        v.wait_ge(sm["kvm"].h, mm_done[i])
                        if kind == "K":
                            for g4 in range(4):
                                for t2 in range(2):
                                    inst = v.tensor_copy(
                                        self.KT_sb[:, g4 * S + p * 1024 + t2 * 512:
                                                   g4 * S + p * 1024 + (t2 + 1) * 512],
                                        psq_slice(psk, g4 * 2 + t2))
                        else:
                            for t8 in range(8):
                                kt = p * 8 + t8
                                inst = v.tensor_copy(
                                    self.V_sb[:, kt * 512:(kt + 1) * 512],
                                    psq_slice(psk, t8))
                        inst.then_inc(sm["kvev"].h, 1)
                    ops["v"].append(kev)
                    ev_done[(0 if kind == "K" else 4) + p] = sm["kvev"].inc()


            with nc.Block() as blk:
                blk.sync(lambda sy: [f(sy) for f in ops["y"]])
                blk.tensor(lambda t: [f(t) for f in ops["t"]])
                blk.vector(lambda v: [f(v) for f in ops["v"]])

            with nc.Block() as blk:
                @blk.tensor
                def _(t):
                    for g4 in range(4):
                        for kt in range(32):
                            inst = t.matmul(psq_slice(psk, g4)[:, 0:1],
                                            self.V_sb[:, kt * 512 + g4 * 128:
                                                      kt * 512 + (g4 + 1) * 128],
                                            self.onescol[:, :],
                                            start=(kt == 0), stop=(kt == 31))
                        inst.then_inc(sm["vsm"].h, 1)

                @blk.vector
                def _(v):
                    v.wait_ge(sm["vsm"].h, sm["vsm"].inc(4))
                    for g4 in range(4):
                        v.tensor_copy(self.Vsum_bf[:, g4:g4 + 1],
                                      psq_slice(psk, g4)[:, 0:1])

    def p3_attention(self):
        nc = self.nc
        sm = self.sems(["a3sc", "a3ex", "a3mk", "a3av", "a3evc", "a3sd0",
                        "a3sd1", "a3rc", "a3rd0", "a3rd1", "a3rd2", "a3rd3",
                        "a3rd4", "a3rd5",
                        "a3rw", "a3nm", "a3qt", "a3od", "a3oe"])
        with ExitStack() as ph:
            QT_sb = self.sbuf(ph, "QT_sb", [128, NQ * QTW], BF16)
            PT = self.sbuf(ph, "PT", [128, 3 * 1024], BF16)
            OTQ = self.sbuf(ph, "OTQ", [128, 2 * NQ * QTW], BF16)
            srow = self.sbuf(ph, "srow", [1, 2 * QTW], F32)
            sums_all = self.sbuf(ph, "sums_all", [32, QTW], F32)
            rec_all = self.sbuf(ph, "rec_all", [32, QTW], F32)
            rbc = self.sbuf(ph, "rbc", [128, 6 * QTW], F32)
            ps_sc = self.psum(ph, "ps_sc", [128, 3072], F32)
            ps_av = self.psum(ph, "ps_av", [128, QTW], F32)
            ps_sum = self.psum(ph, "ps_sum", [1, QTW], F32)

            base = {k: v.n for k, v in sm.items()}
            finals = []

            for body in nc.Switch(engines=self.sw_engines,
                                  index=[self.parity] * 5, n=2):
                for k in sm:
                    sm[k].n = base[k]
                half = body
                ownq = QTILES[half]
                othq = QTILES[1 - half]

                def gblock(pb):
                    if pb < 16:
                        return ownq[pb // 4] * 4 + pb % 4
                    return othq[(pb - 16) // 4] * 4 + pb % 4

                exp_done, msk_done, avq_done, sc_done = {}, {}, {}, {}
                evac_done, sdma_done, nrm_done = {}, {}, {}
                fq = 0
                qt_load_done = {}
                otd_done = {}

                for lt in range(4):
                    gq = ownq[lt]
                    ob = lt % 2 * NQ * QTW
                    full = [pb for pb in range(32) if gblock(pb) >= 4 * gq + 4]
                    partial = [lt * 4, lt * 4 + 1, lt * 4 + 2, lt * 4 + 3]
                    kts = full + partial
                    nq2 = len(kts) // 2
                    quads = [(kts[2 * i], kts[2 * i + 1]) for i in range(nq2)]
                    npart = 2
                    fixq = (gq == 7)

                    if lt == 0:
                        def qtl0(g):
                            g.dma_start(QT_sb[:, :], self.s_q[0]
                                        ).then_inc(sm["a3qt"].h, 16)
                        self._emit(nc.gpsimd, qtl0)
                        qt_load_done[0] = sm["a3qt"].inc(16)

                    for h in range(NQ):
                        u = lt * 32 + h
                        g4 = h // GRP
                        uq0 = fq
                        for qi, (ka, kb) in enumerate(quads):
                            p = fq % 3
                            is_part = qi >= nq2 - npart
                            first, last = qi == 0, qi == nq2 - 1

                            def pe_sc(t, ka=ka, kb=kb, p=p, fq=fq, g4=g4,
                                      lt=lt, u=u, qi=qi):
                                if qi == 0:
                                    t.wait_ge(sm["a3qt"].h, qt_load_done[lt])
                                if fq >= 3:
                                    t.wait_ge(sm["a3ex"].h, exp_done[fq - 3])
                                for i, kt in enumerate((ka, kb)):
                                    inst = t.matmul(
                                        ps_sc[:, p * 1024 + i * 512:
                                              p * 1024 + (i + 1) * 512],
                                        self.KT_sb[:, g4 * S + kt * 128:
                                                   g4 * S + (kt + 1) * 128],
                                        QT_sb[:, h * QTW:(h + 1) * QTW],
                                        start=True, stop=True)
                                inst.then_inc(sm["a3sc"].h, 1)
                            sc_done[fq] = sm["a3sc"].inc()

                            def act_ex(s, p=p, fq=fq):
                                s.wait_ge(sm["a3sc"].h, sc_done[fq])
                                if fq >= 3:
                                    s.wait_ge(sm["a3av"].h, avq_done[fq - 3])
                                s.activation(PT[:, p * 1024:(p + 1) * 1024],
                                             ps_sc[:, p * 1024:(p + 1) * 1024],
                                             AF.Exp, scale=ASCALE
                                             ).then_inc(sm["a3ex"].h, 1)
                            exp_done[fq] = sm["a3ex"].inc()

                            if is_part:
                                mp = 0 if qi == nq2 - 2 else 1

                                def dve_mk(v, p=p, fq=fq, mp=mp):
                                    v.wait_ge(sm["a3ex"].h, exp_done[fq])
                                    v.tensor_tensor(
                                        PT[:, p * 1024:(p + 1) * 1024],
                                        PT[:, p * 1024:(p + 1) * 1024],
                                        self.maskp_bf[:, mp * 1024:(mp + 1) * 1024],
                                        op=ALU.mult).then_inc(sm["a3mk"].h, 1)
                                msk_done[fq] = sm["a3mk"].inc()

                            def pe_av(t, ka=ka, kb=kb, p=p, fq=fq, g4=g4, u=u,
                                      first=first, last=last, is_part=is_part):
                                if is_part:
                                    t.wait_ge(sm["a3mk"].h, msk_done[fq])
                                else:
                                    t.wait_ge(sm["a3ex"].h, exp_done[fq])
                                if first and u >= 1:
                                    t.wait_ge(sm["a3evc"].h, evac_done[u - 1])
                                for i, kt in enumerate((ka, kb)):
                                    t.matmul(ps_av[:, :],
                                             self.V_sb[:, kt * 512 + g4 * 128:
                                                       kt * 512 + (g4 + 1) * 128],
                                             PT[:, p * 1024 + i * 512:
                                                p * 1024 + (i + 1) * 512],
                                             start=(first and i == 0),
                                             stop=(last and i == 1))
                                for i in range(2):
                                    inst = t.matmul(
                                        ps_sum[:, :], self.onescol[:, :],
                                        PT[:, p * 1024 + i * 512:
                                           p * 1024 + (i + 1) * 512],
                                        start=(first and i == 0),
                                        stop=(last and i == 1))
                                inst.then_inc(sm["a3av"].h, 1)
                            avq_done[fq] = sm["a3av"].inc()

                            if qi == 0:
                                self._emit(nc.tensor, pe_sc)
                                self._p3_pend = []
                            elif qi <= 2:
                                self._emit(nc.tensor, pe_sc)
                            self._p3_pend.append((pe_av, pe_sc, qi))
                            self._emit(nc.scalar, act_ex)
                            if is_part:
                                self._emit(nc.vector, dve_mk)
                            fq += 1

                        for qi, (av_f, sc_f, _q) in enumerate(self._p3_pend):
                            self._emit(nc.tensor, av_f)
                            if qi + 3 < nq2:
                                self._emit(nc.tensor,
                                           self._p3_pend[qi + 3][1])

                        def dve_ev(v, u=u, h=h, g4=g4, fixq=fixq, lt=lt, ob=ob):
                            v.wait_ge(sm["a3av"].h, avq_done[uq0 + nq2 - 1])
                            if u >= 2:
                                v.wait_ge(sm["a3sd%d" % (u % 2)].h,
                                          sdma_done[u - 2])
                            if lt > 1 and u % 32 == 0:
                                v.wait_ge(sm["a3o%s" % "de"[lt % 2]].h,
                                          otd_done[lt - 2])
                            v.tensor_copy(OTQ[:, ob + h * QTW:ob + (h + 1) * QTW],
                                          ps_av[:, :])
                            if fixq:
                                v.drain()
                                v.tensor_copy(
                                    OTQ[:, ob + h * QTW + 511:ob + h * QTW + 512],
                                    self.Vsum_bf[:, g4:g4 + 1])
                            inst = v.tensor_copy(
                                srow[:, (u % 2) * QTW:(u % 2 + 1) * QTW],
                                ps_sum[:, :])
                            inst.then_inc(sm["a3evc"].h, 1)
                        self._emit(nc.vector, dve_ev)
                        evac_done[u] = sm["a3evc"].inc()

                        def sy_sr(sy, u=u, h=h):
                            sy.wait_ge(sm["a3evc"].h, evac_done[u])
                            sy.dma_start(sums_all[h:h + 1, :],
                                         srow[:, (u % 2) * QTW:(u % 2 + 1) * QTW]
                                         ).then_inc(sm["a3sd%d" % (u % 2)].h, 16)
                        self._emit(nc.scalar, sy_sr)
                        sdma_done[u] = sm["a3sd%d" % (u % 2)].inc(16)

                    if lt < 3:
                        def qtl(g, lt=lt, scv=sm["a3sc"].n):
                            g.wait_ge(sm["a3sc"].h, scv)
                            g.dma_start(QT_sb[:, :], self.s_q[lt + 1]
                                        ).then_inc(sm["a3qt"].h, 16)
                        self._emit(nc.gpsimd, qtl)
                        qt_load_done[lt + 1] = sm["a3qt"].inc(16)

                    def dve_rc(v, lt=lt, fixq=fixq, u_last=lt * 32 + 31,
                               rwv=sm["a3rw"].n):
                        v.wait_ge(sm["a3rw"].h, rwv)
                        v.wait_ge(sm["a3sd0"].h, sdma_done[u_last - 1])
                        v.wait_ge(sm["a3sd1"].h, sdma_done[u_last])
                        if fixq:
                            v.memset(sums_all[:, 511:512], 4096.0)
                            v.drain()
                        v.reciprocal(rec_all[:, :], sums_all[:, :]
                                     ).then_inc(sm["a3rc"].h, 1)
                    self._emit(nc.vector, dve_rc)
                    rc_done = sm["a3rc"].inc()

                    def sy_rw(sy, rc_done=rc_done,
                              rds=tuple((("a3rd%d" % k), sm["a3rd%d" % k].n)
                                        for k in range(6))):
                        sy.wait_ge(sm["a3rc"].h, rc_done)
                        for nm_, vv in rds:
                            sy.wait_ge(sm[nm_].h, vv)
                        sy.dma_start(self.s_row32, rec_all[:, :]
                                     ).then_inc(sm["a3rw"].h, 16)
                    self._emit(nc.sync, sy_rw)
                    rw_done = sm["a3rw"].inc(16)

                    for h in range(NQ):
                        u = lt * 32 + h

                        def sy_rr(sy, u=u, h=h, rw_done=rw_done):
                            if h == 0:
                                sy.wait_ge(sm["a3rw"].h, rw_done)
                            if h >= 6:
                                sy.wait_ge(sm["a3nm"].h, nrm_done[u - 6])
                            sy.dma_start(
                                rbc[:, (h % 6) * QTW:(h % 6 + 1) * QTW],
                                self.s_row32[h:h + 1, :]
                                .to_broadcast([128, QTW])
                            ).then_inc(sm["a3rd%d" % (h % 6)].h, 16)
                        self._emit(nc.sync, sy_rr)
                        rd_done = sm["a3rd%d" % (h % 6)].inc(16)

                        def gp_nm(g, u=u, h=h, rd_done=rd_done, ob=ob):
                            g.wait_ge(sm["a3rd%d" % (h % 6)].h, rd_done)
                            g.tensor_tensor(OTQ[:, ob + h * QTW:ob + (h + 1) * QTW],
                                            OTQ[:, ob + h * QTW:ob + (h + 1) * QTW],
                                            rbc[:, (h % 6) * QTW:(h % 6 + 1) * QTW],
                                            op=ALU.mult).then_inc(sm["a3nm"].h, 1)
                        self._emit(nc.gpsimd, gp_nm)
                        nrm_done[u] = sm["a3nm"].inc()

                        def sy_ot(sy, u=u, h=h, lt=lt, ob=ob):
                            sy.wait_ge(sm["a3nm"].h, nrm_done[u])
                            for mtl in range(4):
                                sy.dma_start(
                                    self.s_ot[lt * 4 + mtl, :,
                                              h * 128:(h + 1) * 128],
                                    OTQ[:, ob + h * QTW + mtl * 128:
                                        ob + h * QTW + (mtl + 1) * 128]
                                ).then_inc(sm["a3o%s" % "de"[lt % 2]].h, 16)
                        self._emit(nc.sync, sy_ot)
                        sm["a3o%s" % "de"[lt % 2]].inc(64)
                    otd_done[lt] = sm["a3o%s" % "de"[lt % 2]].n

                finals.append({k: v.n for k, v in sm.items()})

                def fin_wait(sy):
                    sy.wait_ge(sm["a3od"].h, sm["a3od"].n)
                    sy.wait_ge(sm["a3oe"].h, sm["a3oe"].n)
                    for k in range(6):
                        sy.wait_ge(sm["a3rd%d" % k].h, sm["a3rd%d" % k].n)
                self._emit(nc.sync, fin_wait)

            assert finals[0] == finals[1], (finals[0], finals[1])
            for k in sm:
                sm[k].n = finals[0][k]
            self._ot_finals = (sm["a3od"].h, sm["a3od"].n,
                               sm["a3oe"].h, sm["a3oe"].n)

        with nc.Block() as blk:
            @blk.sync
            def _(sy):
                pass

    def _emit(self, eng, f):
        f(eng)

    def p4_wo(self):
        nc = self.nc
        NPASS = 4
        PW = D // NPASS
        sm = self.sems(["w4ga", "w4gb", "w4o0", "w4o1", "w4x0", "w4x1", "w4m",
                        "w4e", "w4d0", "w4d1"])
        with ExitStack() as ph:
            Wo_sb = self.sbuf(ph, "Wo_sb", [128, 2 * 32 * PW], BF16)
            otm = self.sbuf(ph, "otm", [128, 2 * 4096], BF16)
            xtl = self.sbuf(ph, "xtl", [128, 2 * PW], F32)
            x2t = self.sbuf(ph, "x2t", [128, 2 * PW], F32)
            psw = self.psum(ph, "psw", [128, 2 * PW], F32)

            ops = {k: [] for k in "tsvyg"}
            ot_done, x_done, mm_done, ev_done, dm_done = {}, {}, {}, {}, {}
            wo_done = {}

            for np_ in range(NPASS):
                wsl = np_ % 2

                def wol(g, np_=np_, wsl=wsl):
                    if np_ >= 2:
                        g.wait_ge(sm["w4m"].h, mm_done[(np_ - 2, NMT - 1)])
                    g.dma_start(Wo_sb[:, wsl * 32 * PW:(wsl + 1) * 32 * PW],
                                self.Wo[np_]
                                ).then_inc(sm["w4g%s" % "ab"[wsl]].h, 16)
                ops["g"].append(wol)
                wo_done[np_] = sm["w4g%s" % "ab"[wsl]].inc(16)

                for mt in range(NMT):
                    i = np_ * NMT + mt
                    ws = i % 2

                    def otl(sy, mt=mt, ws=ws, i=i):
                        if i >= 2:
                            sy.wait_ge(sm["w4m"].h, mm_done[divmod(i - 2, NMT)])
                        sy.dma_start(otm[:, ws * 4096:(ws + 1) * 4096],
                                     self.s_ot[mt, :, :]
                                     ).then_inc(sm["w4o%d" % ws].h, 16)
                    ops["y"].append(otl)
                    ot_done[i] = sm["w4o%d" % ws].inc(16)

                    def xl(sy, np_=np_, mt=mt, ws=ws, i=i):
                        if i >= 2:
                            sy.wait_ge(sm["w4e"].h, ev_done[divmod(i - 2, NMT)])
                        sy.dma_start(xtl[:, ws * PW:(ws + 1) * PW],
                                     self.x_own[mt * 128:(mt + 1) * 128,
                                                np_ * PW:(np_ + 1) * PW]
                                     ).then_inc(sm["w4x%d" % ws].h, 16)
                    ops["y"].append(xl)
                    x_done[i] = sm["w4x%d" % ws].inc(16)

                    def wmm(t, np_=np_, mt=mt, ws=ws, i=i):
                        wsl = np_ % 2
                        t.wait_ge(sm["w4o%d" % ws].h, ot_done[i])
                        if mt == 0:
                            t.wait_ge(sm["w4g%s" % "ab"[wsl]].h, wo_done[np_])
                        if i >= 2:
                            t.wait_ge(sm["w4e"].h, ev_done[divmod(i - 2, NMT)])
                        for fc in range(32):
                            for nt in range(PW // 512):
                                inst = t.matmul(
                                    psw[:, ws * PW + nt * 512:
                                        ws * PW + (nt + 1) * 512],
                                    otm[:, ws * 4096 + fc * 128:
                                        ws * 4096 + (fc + 1) * 128],
                                    Wo_sb[:, (wsl * 32 + fc) * PW + nt * 512:
                                          (wsl * 32 + fc) * PW + (nt + 1) * 512],
                                    start=(fc == 0), stop=(fc == 31))
                        inst.then_inc(sm["w4m"].h, 1)
                    ops["t"].append(wmm)
                    mm_done[(np_, mt)] = sm["w4m"].inc()

                    def wev(v, np_=np_, mt=mt, ws=ws, i=i):
                        v.wait_ge(sm["w4m"].h, mm_done[(np_, mt)])
                        v.wait_ge(sm["w4x%d" % ws].h, x_done[i])
                        if i >= 2:
                            v.wait_ge(sm["w4d%d" % ws].h, dm_done[i - 2])
                        v.tensor_tensor(x2t[:, ws * PW:(ws + 1) * PW],
                                        psw[:, ws * PW:(ws + 1) * PW],
                                        xtl[:, ws * PW:(ws + 1) * PW],
                                        op=ALU.add).then_inc(sm["w4e"].h, 1)
                    ops["v"].append(wev)
                    ev_done[(np_, mt)] = sm["w4e"].inc()

                    def wdm(sy, np_=np_, mt=mt, ws=ws, i=i):
                        sy.wait_ge(sm["w4e"].h, ev_done[(np_, mt)])
                        sy.dma_start(self.s_x2[mt * 128:(mt + 1) * 128,
                                               np_ * PW:(np_ + 1) * PW],
                                     x2t[:, ws * PW:(ws + 1) * PW]
                                     ).then_inc(sm["w4d%d" % ws].h, 16)
                    ops["y"].append(wdm)
                    dm_done[i] = sm["w4d%d" % ws].inc(16)

            with nc.Block() as blk:
                blk.gpsimd(lambda g: [f(g) for f in ops["g"]])
                blk.sync(lambda sy: [f(sy) for f in ops["y"]])
                blk.tensor(lambda t: [f(t) for f in ops["t"]])
                blk.vector(lambda v: [f(v) for f in ops["v"]])

            with nc.Block() as blk:
                f0, f1 = sm["w4d0"].n, sm["w4d1"].n

                @blk.sync
                def _(sy):
                    sy.wait_ge(sm["w4d0"].h, f0)
                    sy.wait_ge(sm["w4d1"].h, f1)

    def p4c_ln2(self):
        def src_of(tt):
            return self.s_x2[tt * 128:(tt + 1) * 128, :]

        bu = self

        def save_stats(v, tt, agg):
            v.tensor_copy(bu.mean2_sb[:, tt:tt + 1], agg[:, 0:1])
            v.drain()
            v.tensor_copy(bu.rstd2_sb[:, tt:tt + 1], agg[:, 1:2])
            v.drain()

        self._ln_transpose("l2", NMT, src_of, self.s_ln2T, 0,
                           save_stats=save_stats, qtile_blocked=True)

        nc = self.nc
        sm = self.sems(["c2a", "c2b"])
        with ExitStack() as ph:
            tmp = self.sbuf(ph, "c2tmp", [128, NMT], F32)
            with nc.Block() as blk:
                @blk.vector
                def _(v):
                    v.tensor_scalar(tmp[:, :], self.rstd2_sb[:, :],
                                    self.cst[:, CEPS:CEPS + 1], None, op0=ALU.add
                                    ).then_inc(sm["c2a"].h, 1)

                @blk.scalar
                def _(s):
                    s.wait_ge(sm["c2a"].h, sm["c2a"].inc(1))
                    s.activation(tmp[:, :], tmp[:, :], AF.Sqrt
                                 ).then_inc(sm["c2b"].h, 1)

            with nc.Block() as blk:
                @blk.vector
                def _(v):
                    v.reciprocal(tmp[:, :], tmp[:, :])
                    v.drain()
                    v.tensor_tensor(self.logv_sb[:, :], tmp[:, :], tmp[:, :],
                                    op=ALU.mult)
                    v.drain()
                    v.tensor_tensor(self.logv_sb[:, :], self.logv_sb[:, :],
                                    self.rstd2_sb[:, :], op=ALU.mult)
                    v.drain()
                    v.tensor_copy(self.rstd2_sb[:, :], tmp[:, :]
                                  ).then_inc(sm["c2a"].h, 1)

                @blk.scalar
                def _(s):
                    s.wait_ge(sm["c2a"].h, sm["c2a"].inc(1))
                    s.activation(self.logv_sb[:, :], self.logv_sb[:, :],
                                 AF.Ln, bias=1.0)

    def p5_ffn(self):
        nc = self.nc
        sm = self.sems(["f5l", "f5g", "f5a", "f5b", "f5c", "f5d", "f5e",
                        "f5w0", "f5w1", "f5w2", "f5zd", "f5ld", "f5sel", "f5tr",
                        "f5am", "f5ae", "f5zm", "f5ze", "f5zp", "f5zv",
                        "f5hm", "f5hb", "f5lm", "f5le"])
        with ExitStack() as ph:
            auxv_sb = self.sbuf(ph, "auxv_sb", [128, 8 * HID], BF16)
            WvdT_sb = self.sbuf(ph, "WvdT_sb", [128, 32 * VB], BF16)
            WvuT_sb = self.sbuf(ph, "WvuT_sb", [128, 2 * HID], BF16)
            l2t = self.sbuf(ph, "l2t", [128, 32 * 512], BF16)
            qT_bf = self.sbuf(ph, "qT_bf", [BOT, 512], BF16)
            rq_bf = self.sbuf(ph, "rq_bf", [RDIM, 512], BF16)
            R_sb = self.sbuf(ph, "R_sb", [128, NSLOT], F32)
            A_sb = self.sbuf(ph, "A_sb", [128, NSLOT], F32)
            Rz_sb = self.sbuf(ph, "Rz_sb", [128, NSLOT], F32)
            T8 = self.sbuf(ph, "T8", [128, 8], F32)
            Msel = self.sbuf(ph, "Msel", [128, NSLOT], U8)
            Am_sb = self.sbuf(ph, "Am_sb", [128, NSLOT], F32)
            nmx = self.sbuf(ph, "nmx", [128, 1], F32)
            den = self.sbuf(ph, "den", [128, 1], F32)
            E_bf = self.sbuf(ph, "E_bf", [128, NSLOT], BF16)
            wT_q = self.sbuf(ph, "wT_q", [128, 8 * 512], BF16)
            ao_bf = self.sbuf(ph, "ao_bf", [128, 2 * 512], BF16)
            z1_bf = self.sbuf(ph, "z1_bf", [128, 1024], BF16)
            z2_bf = self.sbuf(ph, "z2_bf", [128, 4096], BF16)
            wu1t = self.sbuf(ph, "wu1t", [128, 3 * 1024], BF16)
            h1_bf = self.sbuf(ph, "h1_bf", [128, 4096], BF16)
            lrow = self.sbuf(ph, "lrow", [1, 512], F32)

            with nc.Block() as blk:
                @blk.gpsimd
                def _(g):
                    g.dma_start(auxv_sb[:, :], self.auxv
                                ).then_inc(sm["f5g"].h, 16)
                    g.dma_start(WvdT_sb[:, :], self.WvdT
                                ).then_inc(sm["f5g"].h, 16)
                    g.dma_start(WvuT_sb[:, :], self.WvuT
                                ).then_inc(sm["f5g"].h, 16)
                    g.wait_ge(sm["f5g"].h, sm["f5g"].inc(3 * 16))

            for qt in range(4):
                self._p5_qtile(qt, sm, auxv_sb, WvdT_sb, WvuT_sb, l2t, qT_bf,
                               rq_bf, R_sb, A_sb, Rz_sb, T8, Msel, Am_sb, nmx,
                               den, E_bf, wT_q, ao_bf, z1_bf, z2_bf, wu1t,
                               h1_bf, lrow)

            with nc.Block() as blk:
                fz, fl = sm["f5zd"].n, sm["f5ld"].n

                @blk.sync
                def _(sy):
                    sy.wait_ge(sm["f5zd"].h, fz)
                    sy.wait_ge(sm["f5ld"].h, fl)

    def _p5_qtile(self, qt, sm, auxv_sb, WvdT_sb, WvuT_sb, l2t, qT_bf, rq_bf,
                  R_sb, A_sb, Rz_sb, T8, Msel, Am_sb, nmx, den, E_bf, wT_q,
                  ao_bf, z1_bf, z2_bf, wu1t, h1_bf, lrow):
        nc = self.nc

        with ExitStack() as ps:
            ps_q = self.psum(ps, "p5q_%d" % qt, [BOT, 512])
            ps_r = self.psum(ps, "p5r_%d" % qt, [128, 2048])
            ps_w = self.psum(ps, "p5w_%d" % qt, [128, 1024], BF16)

            with nc.Block() as blk:
                @blk.sync
                def _(sy):
                    sy.dma_start(l2t[:, :], self.s_ln2T[qt]
                                 ).then_inc(sm["f5l"].h, 16)
                    sy.wait_ge(sm["f5l"].h, sm["f5l"].n + 16)
                sm["f5l"].inc(16)

            with nc.Block() as blk:
                @blk.tensor
                def _(t):
                    for dc in range(32):
                        inst = t.matmul(ps_q[:, :],
                                        self.WqFT_sb[:, dc * BOT:(dc + 1) * BOT],
                                        l2t[:, dc * 512:(dc + 1) * 512],
                                        start=(dc == 0), stop=(dc == 31))
                    inst.then_inc(sm["f5a"].h, 1)

                @blk.vector
                def _(v):
                    v.wait_ge(sm["f5a"].h, sm["f5a"].inc(1))
                    v.tensor_copy(qT_bf[:, :], ps_q[:, :]).then_inc(sm["f5b"].h, 1)

            with nc.Block() as blk:
                @blk.tensor
                def _(t):
                    t.matmul(ps_q[0:RDIM, :], self.WrT_sb[:, :], qT_bf[:, :],
                             start=True, stop=True).then_inc(sm["f5a"].h, 1)

                @blk.vector
                def _(v):
                    v.wait_ge(sm["f5a"].h, sm["f5a"].inc(1))
                    v.tensor_copy(rq_bf[:, :], ps_q[0:RDIM, :]
                                  ).then_inc(sm["f5b"].h, 1)

            for tt in range(4):
                with nc.Block() as blk:
                    @blk.tensor
                    def _(t, tt=tt):
                        for i in range(2):
                            t.matmul(ps_r[:, i * 512:(i + 1) * 512],
                                     rq_bf[:, tt * 128:(tt + 1) * 128],
                                     self.rkT_sb[:, i * 512:(i + 1) * 512],
                                     start=True, stop=True)
                        for i in range(2):
                            inst = t.matmul(ps_r[:, 1024 + i * 512:
                                            1024 + (i + 1) * 512],
                                            qT_bf[:, tt * 128:(tt + 1) * 128],
                                            self.auxkT_sb[:, i * 512:(i + 1) * 512],
                                            start=True, stop=True)
                        inst.then_inc(sm["f5a"].h, 1)

                    @blk.vector
                    def _(v, tt=tt):
                        v.wait_ge(sm["f5a"].h, sm["f5a"].inc(1))
                        v.scalar_tensor_tensor(R_sb[:, :], ps_r[:, 0:1024],
                                               RSCALE, self.logrelbc[:, :],
                                               op0=ALU.mult, op1=ALU.add)
                        v.scalar_tensor_tensor(A_sb[:, :], ps_r[:, 1024:2048],
                                               FSCALE, self.logrelbc[:, :],
                                               op0=ALU.mult, op1=ALU.add)
                        v.drain()
                        v.max(out=T8[:, :], in_=R_sb[:, :])
                        v.drain()
                        v.match_replace(out=Rz_sb[:, :], in_to_replace=T8[:, :],
                                        in_values=R_sb[:, :], imm_value=-1e30)
                        v.drain()
                        v.tensor_tensor(Msel[:, :], Rz_sb[:, :], R_sb[:, :],
                                        op=ALU.is_lt)
                        v.drain()
                        v.select(Am_sb[:, :], Msel[:, :], A_sb[:, :],
                                 self.negc[:, :], add_drain=True)
                        v.drain()
                        v.tensor_reduce(nmx[:, :], Am_sb[:, :], axis=AX.X,
                                        op=ALU.max, negate=True
                                        ).then_inc(sm["f5sel"].h, 1)

                    @blk.scalar
                    def _(s, tt=tt):
                        s.wait_ge(sm["f5sel"].h, sm["f5sel"].inc(1))
                        s.activation(E_bf[:, :], Am_sb[:, :], AF.Exp,
                                     scale=1.0, bias=nmx[:, :],
                                     accum_out=den[:, :]).then_inc(sm["f5c"].h, 1)

                with nc.Block() as blk:
                    @blk.vector
                    def _(v, tt=tt):
                        v.reciprocal(den[:, :], den[:, :])
                        v.drain()
                        v.tensor_scalar(E_bf[:, :], E_bf[:, :], den[:, :], None,
                                        op0=ALU.mult).then_inc(sm["f5d"].h, 1)

                    @blk.tensor
                    def _(t, tt=tt):
                        t.wait_ge(sm["f5d"].h, sm["f5d"].n + 1)
                        for sc in range(8):
                            inst = t.transpose(
                                ps_w[:, sc * 128:(sc + 1) * 128],
                                E_bf[:, sc * 128:(sc + 1) * 128],
                                self.id_bf[:, :])
                        inst.then_inc(sm["f5tr"].h, 1)
                    sm["f5d"].inc(1)

                    @blk.vector
                    def _(v, tt=tt):
                        v.wait_ge(sm["f5tr"].h, sm["f5tr"].inc(1))
                        for sc in range(8):
                            v.tensor_copy(
                                wT_q[:, sc * 512 + tt * 128:
                                     sc * 512 + (tt + 1) * 128],
                                ps_w[:, sc * 128:(sc + 1) * 128])

        self._p5_aux_z1(qt, sm, auxv_sb, WvdT_sb, wT_q, ao_bf, z1_bf)

        self._p5_z2(qt, sm, z1_bf, WvuT_sb, z2_bf)

        self._p5_h1(qt, sm, l2t, wu1t, h1_bf, lrow)

    def _p5_aux_z1(self, qt, sm, auxv_sb, WvdT_sb, wT_q, ao_bf, z1_bf):
        nc = self.nc
        with ExitStack() as ps:
            ps_ao = self.psum(ps, "p5ao_%d" % qt, [128, 1024])
            ps_z1 = self.psum(ps, "p5z1_%d" % qt, [128, 1024])
            am_done, ae_done = {}, {}
            pe_ops, dv_ops = [], []
            for dc in range(32):
                sl = dc % 2

                def pe_aux(t, dc=dc, sl=sl):
                    if dc >= 2:
                        t.wait_ge(sm["f5ae"].h, ae_done[dc - 2])
                    for sc in range(8):
                        inst = t.matmul(
                            ps_ao[:, sl * 512:(sl + 1) * 512],
                            auxv_sb[:, sc * HID + dc * 128:
                                    sc * HID + (dc + 1) * 128],
                            wT_q[:, sc * 512:(sc + 1) * 512],
                            start=(sc == 0), stop=(sc == 7))
                    inst.then_inc(sm["f5am"].h, 1)
                am_done[dc] = sm["f5am"].inc()

                def dv_ev(v, dc=dc, sl=sl):
                    v.wait_ge(sm["f5am"].h, am_done[dc])
                    v.tensor_copy(ao_bf[:, sl * 512:(sl + 1) * 512],
                                  ps_ao[:, sl * 512:(sl + 1) * 512]
                                  ).then_inc(sm["f5ae"].h, 1)
                dv_ops.append(dv_ev)
                ae_done[dc] = sm["f5ae"].inc()

                def pe_z1(t, dc=dc, sl=sl):
                    t.wait_ge(sm["f5ae"].h, ae_done[dc])
                    for m in range(2):
                        inst = t.matmul(
                            ps_z1[:, m * 512:(m + 1) * 512],
                            WvdT_sb[:, dc * VB + m * 128:dc * VB + (m + 1) * 128],
                            ao_bf[:, sl * 512:(sl + 1) * 512],
                            start=(dc == 0), stop=(dc == 31))
                    if dc == 31:
                        inst.then_inc(sm["f5zm"].h, 1)
                pe_ops.append((pe_aux, pe_z1))
            zm_done = sm["f5zm"].inc()

            def dv_z1(v):
                v.wait_ge(sm["f5zm"].h, zm_done)
                v.tensor_copy(z1_bf[:, :], ps_z1[:, :]).then_inc(sm["f5ze"].h, 1)
            ze_done = sm["f5ze"].inc()

            with nc.Block() as blk:
                def pe_all(t):
                    for dc, (pa, pz) in enumerate(pe_ops):
                        pa(t)
                        if dc >= 1:
                            pe_ops[dc - 1][1](t)
                    pe_ops[31][1](t)
                blk.tensor(pe_all)

                def dv_all(v):
                    for f in dv_ops:
                        f(v)
                    dv_z1(v)
                blk.vector(dv_all)

    def _p5_z2(self, qt, sm, z1_bf, WvuT_sb, z2_bf):
        nc = self.nc
        with ExitStack() as ps:
            ps_z2 = self.psum(ps, "p5z2_%d" % qt, [128, 4096])
            base_zd = sm["f5zd"].n
            mm_done = {tt: sm["f5zp"].inc() for tt in range(4)}
            ev_done = {tt: sm["f5zv"].inc() for tt in range(4)}
            dm_done = {tt: sm["f5zd"].inc(16) for tt in range(4)}
            with nc.Block() as blk:
                def pe_all(t):
                    for tt in range(4):
                        if tt > 0:
                            t.wait_ge(sm["f5zv"].h, ev_done[tt - 1])
                        for m in range(2):
                            for nt in range(8):
                                inst = t.matmul(
                                    ps_z2[:, nt * 512:(nt + 1) * 512],
                                    z1_bf[:, m * 512 + tt * 128:
                                          m * 512 + (tt + 1) * 128],
                                    WvuT_sb[:, m * HID + nt * 512:
                                            m * HID + (nt + 1) * 512],
                                    start=(m == 0), stop=(m == 1))
                        inst.then_inc(sm["f5zp"].h, 1)
                blk.tensor(pe_all)

                def dv_all(v):
                    for tt in range(4):
                        v.wait_ge(sm["f5zp"].h, mm_done[tt])
                        if tt >= 1:
                            v.wait_ge(sm["f5zd"].h, dm_done[tt - 1])
                        else:
                            v.wait_ge(sm["f5zd"].h, base_zd)
                        v.tensor_copy(z2_bf[:, :], ps_z2[:, :]
                                      ).then_inc(sm["f5zv"].h, 1)
                blk.vector(dv_all)

                def sy_all(sy):
                    for tt in range(4):
                        sy.wait_ge(sm["f5zv"].h, ev_done[tt])
                        r0 = qt * 512 + tt * 128
                        sy.dma_start(self.s_z2[r0:r0 + 128, :], z2_bf[:, :]
                                     ).then_inc(sm["f5zd"].h, 16)
                blk.sync(sy_all)

    def _p5_h1(self, qt, sm, l2t, wu1t, h1_bf, lrow):
        nc = self.nc
        with ExitStack() as ps:
            ps_h = self.psum(ps, "p5h_%d" % qt, [128, 4096])
            w_done = {dc: sm["f5w%d" % (dc % 3)].inc(16) for dc in range(32)}
            hm_done = {dc: sm["f5hm"].inc() for dc in range(32)}
            with nc.Block() as blk:
                def g_all(g):
                    for dc in range(32):
                        sl = dc % 3
                        if dc >= 3:
                            g.wait_ge(sm["f5hm"].h, hm_done[dc - 3])
                        g.dma_start(wu1t[:, sl * 1024:(sl + 1) * 1024],
                                    self.Wu1T[dc * 128:(dc + 1) * 128, :]
                                    ).then_inc(sm["f5w%d" % sl].h, 16)
                blk.gpsimd(g_all)

                def pe_all(t):
                    for dc in range(32):
                        sl = dc % 3
                        t.wait_ge(sm["f5w%d" % sl].h, w_done[dc])
                        for m8 in range(8):
                            inst = t.matmul(
                                ps_h[:, m8 * 512:(m8 + 1) * 512],
                                wu1t[:, sl * 1024 + m8 * 128:
                                     sl * 1024 + (m8 + 1) * 128],
                                l2t[:, dc * 512:(dc + 1) * 512],
                                start=(dc == 0), stop=(dc == 31))
                        inst.then_inc(sm["f5hm"].h, 1)
                blk.tensor(pe_all)

                def s_all(s):
                    s.wait_ge(sm["f5hm"].h, hm_done[31])
                    for m8 in range(8):
                        s.activation(h1_bf[:, m8 * 512:(m8 + 1) * 512],
                                     ps_h[:, m8 * 512:(m8 + 1) * 512],
                                     AF.Gelu_apprx_tanh, scale=1.0,
                                     bias=self.bu1_sb[:, m8:m8 + 1]
                                     ).then_inc(sm["f5hb"].h, 1)
                    sm["f5hb"].inc(8)
                blk.scalar(s_all)

        with ExitStack() as ps:
            ps_l = self.psum(ps, "p5l_%d" % qt, [1, 512])
            with nc.Block() as blk:
                @blk.tensor
                def _(t):
                    for m8 in range(8):
                        inst = t.matmul(ps_l[:, :], self.Wu2T_sb[:, m8:m8 + 1],
                                        h1_bf[:, m8 * 512:(m8 + 1) * 512],
                                        start=(m8 == 0), stop=(m8 == 7))
                    inst.then_inc(sm["f5lm"].h, 1)

                @blk.vector
                def _(v):
                    v.wait_ge(sm["f5lm"].h, sm["f5lm"].inc(1))
                    v.tensor_copy(lrow[:, :], ps_l[:, :]).then_inc(sm["f5le"].h, 1)

                @blk.sync
                def _(sy):
                    sy.wait_ge(sm["f5le"].h, sm["f5le"].inc(1))
                    sy.dma_start(self.s_lr[0:1, qt * 512:(qt + 1) * 512],
                                 lrow[:, :]).then_inc(sm["f5ld"].h, 16)
                    sy.wait_ge(sm["f5ld"].h, sm["f5ld"].inc(16))

    def p6_collective(self):
        nc = self.nc
        sm = self.sems(["c6m", "c6e", "c6d", "c6c"])
        with ExitStack() as ph:
            srow = self.sbuf(ph, "c6row", [1, 16], F32)
            gs = self.sbuf(ph, "c6gs", [1, 4], F32)
            rgm = self.sbuf(ph, "c6rgm", [128, 1], F32)
            ps6 = self.psum(ph, "ps6", [1, 16], F32)
            with nc.Block() as blk:
                @blk.tensor
                def _(t):
                    t.matmul(ps6[:, :], self.onescolf[:, :], self.logv_sb[:, :],
                             start=True, stop=True).then_inc(sm["c6m"].h, 1)

                @blk.vector
                def _(v):
                    v.wait_ge(sm["c6m"].h, sm["c6m"].inc(1))
                    v.memset(gs[:, :], 0.0)
                    v.drain()
                    v.tensor_reduce(gs[:, 0:1], ps6[:, :], axis=AX.X,
                                    op=ALU.add).then_inc(sm["c6e"].h, 1)

                @blk.sync
                def _(sy):
                    sy.wait_ge(sm["c6e"].h, sm["c6e"].inc(1))
                    sy.dma_start(self.s_cc_in, gs[:, :]).then_inc(sm["c6d"].h, 16)
                    sy.wait_ge(sm["c6d"].h, sm["c6d"].inc(16))

            with nc.Block() as blk:
                @blk.gpsimd
                def _(g):
                    if _SIM_SINGLE:
                        g.dma_start(self.s_cc_out, self.s_cc_in
                                    ).then_inc(sm["c6c"].h, 16)
                        g.wait_ge(sm["c6c"].h, sm["c6c"].inc(16))
                    else:
                        g.collective_compute(
                            "AllReduce", mybir.AluOpType.add,
                            replica_groups=[[0, 1, 2, 3, 4, 5, 6, 7]],
                            ins=[self.s_cc_in.opt()],
                            outs=[self.s_cc_out.opt()],
                        ).then_inc(sm["c6c"].h, 1)
                        g.wait_ge(sm["c6c"].h, sm["c6c"].inc(1))

            with nc.Block() as blk:
                @blk.sync
                def _(sy):
                    sy.dma_start(gs[:, :], self.s_cc_out).then_inc(sm["c6d"].h, 16)
                    sy.wait_ge(sm["c6d"].h, sm["c6d"].inc(16))

            with nc.Block() as blk:
                @blk.vector
                def _(v):
                    v.tensor_scalar(gs[:, 0:1], gs[:, 0:1],
                                    self.cst[0:1, C16K:C16K + 1],
                                    self.cst[0:1, C1M6:C1M6 + 1],
                                    op0=ALU.mult, op1=ALU.add)
                    v.drain()
                    v.reciprocal(gs[:, 0:1], gs[:, 0:1])
                    v.drain()
                    v.tensor_scalar(gs[:, 0:1], gs[:, 0:1],
                                    self.cst[0:1, CHALF:CHALF + 1], None,
                                    op0=ALU.mult).then_inc(sm["c6e"].h, 1)

                @blk.sync
                def _(sy):
                    sy.wait_ge(sm["c6e"].h, sm["c6e"].inc(1))
                    sy.dma_start(self.s_row[2:3, 0:4], gs[:, :]
                                 ).then_inc(sm["c6d"].h, 16)
                    sy.wait_ge(sm["c6d"].h, sm["c6d"].inc(16))
                    sy.dma_start(rgm[:, :], self.s_row[2:3, 0:1]
                                 .to_broadcast([128, 1])).then_inc(sm["c6d"].h, 16)
                    sy.wait_ge(sm["c6d"].h, sm["c6d"].inc(16))

            with nc.Block() as blk:
                @blk.vector
                def _(v):
                    v.tensor_scalar(self.nvh_sb[:, :], self.logv_sb[:, :],
                                    rgm[:, :], None, op0=ALU.mult)

    def p7_final(self):
        nc = self.nc
        sm = self.sems(["o7x0", "o7x1", "o7z0", "o7z1", "o7l0", "o7l1", "o7g",
                        "o7c", "o7d0", "o7d1", "o7s"])
        with ExitStack() as ph:
            x2t = self.sbuf(ph, "o7x2", [128, 2 * D], F32)
            z2t = self.sbuf(ph, "o7z2", [128, 2 * D], BF16)
            lcol = self.sbuf(ph, "o7lc", [128, 1], F32)
            sig = self.sbuf(ph, "o7sig", [128, 1], F32)
            u_c = self.sbuf(ph, "o7u", [128, 1], F32)
            gate = self.sbuf(ph, "o7gate", [128, 1], F32)
            mex = self.sbuf(ph, "o7mex", [128, 1], F32)
            a1 = self.sbuf(ph, "o7a1", [128, 1], F32)
            a0 = self.sbuf(ph, "o7a0", [128, 1], F32)
            t1 = self.sbuf(ph, "o7t1", [128, 2 * D], F32)

            ops = {k: [] for k in "tsvyg"}
            x_done, z_done, l_done, g_done, c_done, d_done = {}, {}, {}, {}, {}, {}
            s_done = {}
            for mt in range(NMT):
                ws = mt % 2

                def xl(sy, mt=mt, ws=ws):
                    if mt >= 2:
                        sy.wait_ge(sm["o7c"].h, c_done[mt - 2])
                    sy.dma_start(x2t[:, ws * D:(ws + 1) * D],
                                 self.s_x2[mt * 128:(mt + 1) * 128, :]
                                 ).then_inc(sm["o7x%d" % ws].h, 16)
                    sy.dma_start(z2t[:, ws * D:(ws + 1) * D],
                                 self.s_z2[mt * 128:(mt + 1) * 128, :]
                                 ).then_inc(sm["o7z%d" % ws].h, 16)
                    sy.dma_start(lcol[:, :],
                                 self.s_lr[0:1, mt * 128:(mt + 1) * 128]
                                 .rearrange("a (p c) -> (a p) c", c=1),
                                 allow_slow_non_contiguous=True
                                 ).then_inc(sm["o7l%d" % ws].h, 16)
                ops["y"].append(xl)
                x_done[mt] = sm["o7x%d" % ws].inc(16)
                z_done[mt] = sm["o7z%d" % ws].inc(16)
                l_done[mt] = sm["o7l%d" % ws].inc(16)

                def gsig(s, mt=mt, ws=ws):
                    s.wait_ge(sm["o7l%d" % ws].h, l_done[mt])
                    if mt >= 1:
                        s.wait_ge(sm["o7g"].h, g_done[mt - 1])
                    s.activation(sig[:, :], lcol[:, :], AF.Sigmoid,
                                 scale=0.5, bias=self.bu2c[:, :]
                                 ).then_inc(sm["o7s"].h, 1)
                ops["s"].append(gsig)
                sg_v = sm["o7s"].inc()

                def gveg(v, mt=mt, sg_v=sg_v):
                    v.wait_ge(sm["o7s"].h, sg_v)
                    v.scalar_tensor_tensor(u_c[:, :], sig[:, :], 2.5,
                                           self.nvh_sb[:, mt:mt + 1],
                                           op0=ALU.mult, op1=ALU.add)
                    v.drain()
                    v.tensor_scalar(u_c[:, :], u_c[:, :],
                                    self.cst[:, C5:C5 + 1],
                                    self.cst[:, C0:C0 + 1],
                                    op0=ALU.min, op1=ALU.max)
                    v.drain()
                    v.tensor_scalar(u_c[:, :], u_c[:, :],
                                    self.cst[:, CHALF:CHALF + 1],
                                    self.cst[:, C1P5:C1P5 + 1],
                                    op0=ALU.subtract, op1=ALU.mult)
                    v.drain()
                    v.tensor_scalar(u_c[:, :], u_c[:, :],
                                    self.cst[:, C1:C1 + 1],
                                    self.cst[:, C0:C0 + 1],
                                    op0=ALU.min, op1=ALU.max)
                    v.drain()
                    v.scalar_tensor_tensor(u_c[:, :], u_c[:, :], self.gw1c[:, :],
                                           self.gbc[:, :],
                                           op0=ALU.mult, op1=ALU.add
                                           ).then_inc(sm["o7s"].h, 1)
                ops["v"].append(gveg)
                sg2_v = sm["o7s"].inc()

                def gsig2(s, mt=mt, sg2_v=sg2_v):
                    s.wait_ge(sm["o7s"].h, sg2_v)
                    s.activation(gate[:, :], u_c[:, :], AF.Sigmoid
                                 ).then_inc(sm["o7s"].h, 1)
                ops["s"].append(gsig2)
                sg3_v = sm["o7s"].inc()

                def gv2(v, mt=mt, ws=ws, sg3_v=sg3_v):
                    v.wait_ge(sm["o7s"].h, sg3_v)
                    v.tensor_scalar(mex[:, :], gate[:, :],
                                    self.cst[:, CEXIT:CEXIT + 1], None,
                                    op0=ALU.is_ge)
                    v.drain()
                    v.tensor_tensor(gate[:, :], gate[:, :], mex[:, :],
                                    op=ALU.mult)
                    v.drain()
                    v.tensor_scalar(a1[:, :], self.rstd2_sb[:, mt:mt + 1],
                                    self.cst[:, C1:C1 + 1], None, op0=ALU.add)
                    v.tensor_tensor(a0[:, :], self.mean2_sb[:, mt:mt + 1],
                                    self.rstd2_sb[:, mt:mt + 1], op=ALU.mult)
                    v.drain()
                    v.tensor_scalar(a0[:, :], a0[:, :],
                                    self.cst[:, CM1:CM1 + 1], None, op0=ALU.mult
                                    ).then_inc(sm["o7g"].h, 1)
                ops["v"].append(gv2)
                g_done[mt] = sm["o7g"].inc()

                def comb(v, mt=mt, ws=ws):
                    v.wait_ge(sm["o7g"].h, g_done[mt])
                    v.wait_ge(sm["o7x%d" % ws].h, x_done[mt])
                    v.wait_ge(sm["o7z%d" % ws].h, z_done[mt])
                    if mt >= 2:
                        v.wait_ge(sm["o7d%d" % ws].h, d_done[mt - 2])
                    v.tensor_scalar(t1[:, ws * D:(ws + 1) * D],
                                    x2t[:, ws * D:(ws + 1) * D],
                                    a1[:, :], a0[:, :],
                                    op0=ALU.mult, op1=ALU.add)
                    v.drain()
                    v.scalar_tensor_tensor(t1[:, ws * D:(ws + 1) * D],
                                           z2t[:, ws * D:(ws + 1) * D],
                                           gate[:, :],
                                           t1[:, ws * D:(ws + 1) * D],
                                           op0=ALU.mult, op1=ALU.add
                                           ).then_inc(sm["o7c"].h, 1)
                ops["v"].append(comb)
                c_done[mt] = sm["o7c"].inc()

                def odm(sy, mt=mt, ws=ws):
                    sy.wait_ge(sm["o7c"].h, c_done[mt])
                    sy.dma_start(self.out_own[mt * 128:(mt + 1) * 128, :],
                                 t1[:, ws * D:(ws + 1) * D]
                                 ).then_inc(sm["o7d%d" % ws].h, 16)
                ops["y"].append(odm)
                d_done[mt] = sm["o7d%d" % ws].inc(16)

            with nc.Block() as blk:
                blk.sync(lambda sy: [f(sy) for f in ops["y"]])
                blk.scalar(lambda s: [f(s) for f in ops["s"]])
                blk.vector(lambda v: [f(v) for f in ops["v"]])

            with nc.Block() as blk:
                f0, f1 = sm["o7d0"].n, sm["o7d1"].n

                @blk.sync
                def _(sy):
                    sy.wait_ge(sm["o7d0"].h, f0)
                    sy.wait_ge(sm["o7d1"].h, f1)

    def build_all(self):
        self.declare_io()
        sw_engines = [self.nc.tensor, self.nc.scalar, self.nc.vector,
                      self.nc.sync, self.nc.gpsimd]
        pid = self.nc.partition_id([e.engine for e in sw_engines])
        self.parity = pid % 2
        self.sw_engines = sw_engines
        with self.es:
            self.persistent_sbuf()
            self.p0_consts()
            self.p1a_ln1()
            self.p2_q()
            with ExitStack() as kv_stack:
                self.p1b_kv(kv_stack)
                self.p3_attention()
            self.p4_wo()
            self.p4c_ln2()
            self.p5_ffn()
            self.p6_collective()
            self.p7_final()
        return self.nc


def psq_slice(ps, k):
    return ps[:, k * 512:(k + 1) * 512]


def build():
    return Builder().build_all()


_NC_CACHE = {}


def _host_prep(inp):
    f32 = np.float32
    d = {}
    d["Wq"] = np.ascontiguousarray(
        np.asarray(inp["L0_attn_Wq"], f32).reshape(32, 128, 16, 256)
        .transpose(2, 0, 1, 3))
    d["Wk"] = np.ascontiguousarray(
        np.asarray(inp["L0_attn_Wk"], f32).reshape(32, 128, 512)
        .swapaxes(0, 1).reshape(128, 32 * 512))
    d["Wv"] = np.ascontiguousarray(
        np.asarray(inp["L0_attn_Wv"], f32).reshape(32, 128, 512)
        .swapaxes(0, 1).reshape(128, 32 * 512))
    d["Wo"] = np.ascontiguousarray(
        np.asarray(inp["L0_attn_Wo"], f32).reshape(32, 128, 4, 1024)
        .transpose(2, 1, 0, 3).reshape(4, 128, 32 * 1024))
    d["WqFT"] = np.ascontiguousarray(
        np.asarray(inp["L0_ffn_Wq"], f32).T.reshape(32, 128, 64)
        .swapaxes(0, 1).reshape(128, 32 * 64))
    d["WrT"] = np.ascontiguousarray(np.asarray(inp["L0_ffn_Wr"], f32).T)
    d["auxkT"] = np.ascontiguousarray(np.asarray(inp["L0_ffn_aux_keys"], f32).T)
    d["auxv"] = np.ascontiguousarray(
        np.asarray(inp["L0_ffn_aux_values"], f32).reshape(8, 128, 4096)
        .swapaxes(0, 1).reshape(128, 8 * 4096))
    d["WvdT"] = np.ascontiguousarray(
        np.asarray(inp["L0_ffn_Wvd"], f32).T.reshape(32, 128, 256)
        .swapaxes(0, 1).reshape(128, 32 * 256))
    d["WvuT"] = np.ascontiguousarray(
        np.asarray(inp["L0_ffn_Wvu"], f32).T.reshape(2, 128, 4096)
        .swapaxes(0, 1).reshape(128, 2 * 4096))
    d["Wu1T"] = np.ascontiguousarray(np.asarray(inp["L0_ffn_Wu1"], f32).T)
    d["Wu2T"] = np.ascontiguousarray(
        np.asarray(inp["L0_ffn_Wu2"], f32).reshape(1, 8, 128)[0].T)
    d["bu1c"] = np.ascontiguousarray(
        np.asarray(inp["L0_ffn_bu1"], f32).reshape(8, 128).T)
    d["bu2"] = np.asarray(inp["L0_ffn_bu2"], f32).reshape(1, 1)
    d["gw1"] = np.asarray(inp["L0_ffn_gate_w1"], f32).reshape(1, 1)
    d["gbias"] = np.asarray(inp["L0_ffn_gate_bias"], f32).reshape(1, 1)
    d["logrel"] = np.asarray(inp["L0_ffn_log_rel"], f32).reshape(1, NSLOT)
    d["ident"] = np.eye(128, dtype=f32)
    mp = np.zeros((128, 2048), f32)
    k = np.arange(128)[:, None]
    q = np.arange(512)[None, :]
    for j in range(2):
        for i in range(2):
            dd = (2 * j + i) * 128
            mp[:, j * 1024 + i * 512: j * 1024 + (i + 1) * 512] = (
                (k + dd) > q).astype(f32)
    d["maskp"] = mp
    return d


def kernel(**inp):
    x = np.asarray(inp["L0_attn_x"], np.float32)
    shared = _host_prep(inp)
    nc = build()
    in_maps = []
    for c in range(8):
        b, half = c >> 1, c & 1
        qt = QTILES[half]
        oq = QTILES[1 - half]
        xb = x[b].reshape(8, 512, D)
        m = dict(shared)
        m["x_own"] = np.ascontiguousarray(xb[qt].reshape(OWN, D))
        m["x_oth"] = np.ascontiguousarray(xb[oq].reshape(OWN, D))
        in_maps.append(m)
    res = run_bass_kernel_spmd(nc, in_maps, core_ids=list(range(8)))
    out = np.empty((B, S, D), np.float32)
    for c in range(8):
        b, half = c >> 1, c & 1
        qt = QTILES[half]
        oc = np.asarray(res.results[c]["out_own"]).reshape(4, 512, D)
        for j, t in enumerate(qt):
            out[b, t * 512:(t + 1) * 512] = oc[j]
    return out


# version 08 (final)
# speedup vs baseline: 1.0267x; 1.0267x over previous
import numpy as np
from contextlib import ExitStack

import concourse.bass as bass
import concourse.mybir as mybir
from concourse.bass_utils import run_bass_kernel_spmd

F32 = mybir.dt.float32
BF16 = mybir.dt.bfloat16
U8 = mybir.dt.uint8
AF = mybir.ActivationFunctionType
ALU = mybir.AluOpType
AX = mybir.AxisListType

B, S, D = 4, 4096, 4096
NQ, NKV, DK, GRP = 32, 4, 128, 8
QTW = 512
OWN = 2048
NMT = 16
HID, BOT, RDIM, NSLOT, VB = 4096, 64, 48, 1024, 256
EPS = 1e-5
ASCALE = 1.0 / float(np.sqrt(128.0))
RSCALE = 1.0 / float(np.sqrt(48.0))
FSCALE = 1.0 / 8.0
QTILES = ([0, 3, 4, 7], [1, 2, 5, 6])

_DBG = set()
_SIM_SINGLE = False

CEPS, C16K, C1M6, CHALF, C2P5, C5, C1P5, CEXIT, C0, C1, CM1, CNEG = range(12)
CST_VALS = {CEPS: EPS, C16K: 1.0 / 16384.0, C1M6: 1e-6, CHALF: 0.5, C2P5: 2.5,
            C5: 5.0, C1P5: 1.0 / 1.5, CEXIT: 0.05, C0: 0.0, C1: 1.0,
            CM1: -1.0, CNEG: -30000.0}


class SB:
    def __init__(self, h):
        self.h = h
        self.n = 0

    def inc(self, k=1):
        self.n += k
        return self.n


class Builder:
    def __init__(self):
        self.nc = bass.Bass("TRN2", target_bir_lowering=False)
        self.es = ExitStack()
        self._semnames = set()

    def sems(self, names):
        if not hasattr(self, "_sempool"):
            self._sempool = {}
        out = {}
        for n in names:
            if n not in self._sempool:
                self._sempool[n] = SB(self.es.enter_context(self.nc.semaphore(n)))
            out[n] = self._sempool[n]
        return out

    def sbuf(self, stack, name, shape, dt):
        return stack.enter_context(self.nc.sbuf_tensor(name, shape, dt))

    def psum(self, stack, name, shape, dt=F32):
        return stack.enter_context(self.nc.psum_tensor(name, shape, dt))

    def declare_io(self):
        nc = self.nc

        def din(name, shape, dt=F32):
            return nc.dram_tensor(name, shape, dt, kind="ExternalInput").ap()

        def dscr(name, shape, dt):
            kind = "ExternalOutput" if name in _DBG else "Internal"
            return nc.dram_tensor(name, shape, dt, kind=kind).ap()

        self.x_own = din("x_own", [OWN, D])
        self.x_oth = din("x_oth", [OWN, D])
        self.Wq = din("Wq", [16, 32, 128, 256])
        self.Wk = din("Wk", [128, 32 * 512])
        self.Wv = din("Wv", [128, 32 * 512])
        self.Wo = din("Wo", [4, 128, 32 * 1024])
        self.WqFT = din("WqFT", [128, 32 * BOT])
        self.WrT = din("WrT", [BOT, RDIM])
        self.auxkT = din("auxkT", [BOT, NSLOT])
        self.auxv = din("auxv", [128, 8 * HID])
        self.WvdT = din("WvdT", [128, 32 * VB])
        self.WvuT = din("WvuT", [128, 2 * HID])
        self.Wu1T = din("Wu1T", [HID, 1024])
        self.Wu2T = din("Wu2T", [128, 8])
        self.bu1c = din("bu1c", [128, 8])
        self.bu2 = din("bu2", [1, 1])
        self.gw1 = din("gw1", [1, 1])
        self.gbias = din("gbias", [1, 1])
        self.logrel = din("logrel", [1, NSLOT])
        self.ident = din("ident", [128, 128])
        self.maskp = din("maskp", [128, 2048])

        self.out_own = nc.dram_tensor("out_own", [OWN, D], F32,
                                      kind="ExternalOutput").ap()

        self.s_ln1T = dscr("s_ln1T", [32, 128, S], BF16)
        self.s_q = dscr("s_q", [4, 128, NQ * QTW], BF16)
        self.s_ot = dscr("s_ot", [NMT, 128, NQ * 128], BF16)
        self.s_x2 = dscr("s_x2", [OWN, D], F32)
        self.s_ln2T = dscr("s_ln2T", [4, 128, 32 * 512], BF16)
        self.s_z2 = dscr("s_z2", [OWN, D], BF16)
        self.s_lr = dscr("s_lr", [1, OWN], F32)
        self.s_row = dscr("s_row", [4, QTW], F32)
        self.s_row32 = dscr("s_row32", [32, QTW], F32)
        self.s_cc_in = dscr("s_cc_in", [1, 4], F32)
        self.s_cc_out = dscr("s_cc_out", [1, 4], F32)
        self.s_kv = dscr("s_kv", [1, 4], F32)

    def persistent_sbuf(self):
        es = self.es
        self.onescol = self.sbuf(es, "onescol", [128, 1], BF16)
        self.onescolf = self.sbuf(es, "onescolf", [128, 1], F32)
        self.id_bf = self.sbuf(es, "id_bf", [128, 128], BF16)
        self.maskp_bf = self.sbuf(es, "maskp_bf", [128, 2048], BF16)
        self.cst = self.sbuf(es, "cst", [128, 12], F32)
        self.negc = self.sbuf(es, "negc", [128, NSLOT], F32)
        self.logrelbc = self.sbuf(es, "logrelbc", [128, NSLOT], F32)
        self.WqFT_sb = self.sbuf(es, "WqFT_sb", [128, 32 * BOT], BF16)
        self.WrT_sb = self.sbuf(es, "WrT_sb", [BOT, RDIM], BF16)
        self.auxkT_sb = self.sbuf(es, "auxkT_sb", [BOT, NSLOT], BF16)
        self.rkT_sb = self.sbuf(es, "rkT_sb", [RDIM, NSLOT], BF16)
        self.Wu2T_sb = self.sbuf(es, "Wu2T_sb", [128, 8], BF16)
        self.bu1_sb = self.sbuf(es, "bu1_sb", [128, 8], F32)
        self.bu2c = self.sbuf(es, "bu2c", [128, 1], F32)
        self.gw1c = self.sbuf(es, "gw1c", [128, 1], F32)
        self.gbc = self.sbuf(es, "gbc", [128, 1], F32)
        self.Vsum_bf = self.sbuf(es, "Vsum_bf", [128, NKV], BF16)
        self.mean2_sb = self.sbuf(es, "mean2_sb", [128, NMT], F32)
        self.rstd2_sb = self.sbuf(es, "rstd2_sb", [128, NMT], F32)
        self.logv_sb = self.sbuf(es, "logv_sb", [128, NMT], F32)
        self.nvh_sb = self.sbuf(es, "nvh_sb", [128, NMT], F32)

    def p0_consts(self):
        nc = self.nc
        sm = self.sems(["p0g", "p0y", "p0c"])
        with nc.Block() as blk:
            @blk.vector
            def _(v):
                v.memset(self.onescol[:, :], 1.0)
                v.memset(self.onescolf[:, :], 1.0)
                for k, val in CST_VALS.items():
                    v.memset(self.cst[:, k:k + 1], val)
                v.memset(self.negc[:, :], -30000.0)

            @blk.gpsimd
            def _(g):
                g.dma_start(self.id_bf[:, :], self.ident).then_inc(sm["p0g"].h, 16)
                g.dma_start(self.maskp_bf[:, :], self.maskp).then_inc(sm["p0g"].h, 16)
                g.dma_start(self.WqFT_sb[:, :], self.WqFT
                            ).then_inc(sm["p0g"].h, 16)
                g.dma_start(self.WrT_sb[:, :], self.WrT).then_inc(sm["p0g"].h, 16)
                g.dma_start(self.auxkT_sb[:, :], self.auxkT).then_inc(sm["p0g"].h, 16)
                g.dma_start(self.Wu2T_sb[:, :], self.Wu2T
                            ).then_inc(sm["p0g"].h, 16)
                g.wait_ge(sm["p0g"].h, sm["p0g"].inc(6 * 16))

            @blk.sync
            def _(sy):
                sy.dma_start(self.logrelbc[:, :],
                             self.logrel.to_broadcast([128, NSLOT])
                             ).then_inc(sm["p0y"].h, 16)
                sy.dma_start(self.bu1_sb[:, :], self.bu1c).then_inc(sm["p0y"].h, 16)
                sy.dma_start(self.bu2c[:, :], self.bu2.to_broadcast([128, 1])
                             ).then_inc(sm["p0y"].h, 16)
                sy.dma_start(self.gw1c[:, :], self.gw1.to_broadcast([128, 1])
                             ).then_inc(sm["p0y"].h, 16)
                sy.dma_start(self.gbc[:, :], self.gbias.to_broadcast([128, 1])
                             ).then_inc(sm["p0y"].h, 16)
                sy.wait_ge(sm["p0y"].h, sm["p0y"].inc(5 * 16))

        with ExitStack() as ps:
            ps_rk = self.psum(ps, "ps_rk", [RDIM, NSLOT])
            with nc.Block() as blk:
                @blk.tensor
                def _(t):
                    for i in range(2):
                        t.matmul(ps_rk[:, i * 512:(i + 1) * 512], self.WrT_sb[:, :],
                                 self.auxkT_sb[:, i * 512:(i + 1) * 512],
                                 start=True, stop=True).then_inc(sm["p0c"].h, 1)

                @blk.vector
                def _(v):
                    v.wait_ge(sm["p0c"].h, sm["p0c"].inc(2))
                    v.tensor_copy(self.rkT_sb[:, :], ps_rk[:, :])

    def _ln_transpose(self, tag, ntiles, src_of, lnT_scr, tok_base,
                      save_stats=None, qtile_blocked=False):
        nc = self.nc
        names = ["x0", "x1", "x2", "x3", "st", "sq", "ap", "tr", "ev0", "ev1",
                 "dm0", "dm1", "dm2", "dm3"]
        smr = self.sems(["lt" + n for n in names])
        sm = {tag + n: smr["lt" + n] for n in names}
        with ExitStack() as ph:
            xb = self.sbuf(ph, tag + "xb", [128, 4 * D], F32)
            lnb = self.sbuf(ph, tag + "lnb", [128, 8 * D], BF16)
            stt = self.sbuf(ph, tag + "stt", [128, 48], F32)
            agg = self.sbuf(ph, tag + "agg", [128, 8], F32)
            rstd = self.sbuf(ph, tag + "rstd", [128, 4], F32)
            nmr = self.sbuf(ph, tag + "nmr", [128, 4], F32)
            stg = self.sbuf(ph, tag + "stg", [128, 4 * 1024], BF16)
            pst = self.psum(ph, tag + "pst", [128, 4 * 1024], BF16)

            ops = {k: [] for k in "tsvyg"}
            x_done, ap_done, sq_done = {}, {}, {}
            tr_done, ev_done, dm_done = {}, {}, {}
            nsup = ntiles // 4

            for T in range(nsup):
                sp = T % 2
                for sub in range(4):
                    tt = T * 4 + sub
                    xs = tt % 4
                    src = src_of(tt)

                    def ld(sy, src=src, xs=xs, tt=tt):
                        if tt >= 4:
                            sy.wait_ge(sm[tag + "ap"].h, ap_done[tt - 4])
                        sy.dma_start(xb[:, xs * D:(xs + 1) * D], src
                                     ).then_inc(sm[tag + "x%d" % xs].h, 16)
                    ops["y"].append(ld)
                    x_done[tt] = sm[tag + "x%d" % (tt % 4)].inc(16)

                    sl4 = tt % 4

                    def stats(v, xs=xs, tt=tt, T=T, sl4=sl4):
                        v.wait_ge(sm[tag + "x%d" % xs].h, x_done[tt])
                        for ch in range(8):
                            v.bn_stats(stt[:, ch * 6:(ch + 1) * 6],
                                       xb[:, xs * D + ch * 512:
                                          xs * D + (ch + 1) * 512])
                        v.drain()
                        v.bn_aggr(agg[:, sl4 * 2:sl4 * 2 + 2], stt[:, :])
                        v.drain()
                        if save_stats is not None:
                            save_stats(v, tt, agg[:, sl4 * 2:sl4 * 2 + 2])
                        if tt >= 4:
                            v.wait_ge(sm[tag + "ap"].h, ap_done[tt - 4])
                        v.tensor_scalar(rstd[:, sl4:sl4 + 1],
                                        agg[:, sl4 * 2 + 1:sl4 * 2 + 2],
                                        self.cst[:, CEPS:CEPS + 1], None,
                                        op0=ALU.add).then_inc(sm[tag + "st"].h, 1)
                    ops["v"].append(stats)
                    st_v = sm[tag + "st"].inc()

                    def sqr(s, st_v=st_v, sl4=sl4):
                        s.wait_ge(sm[tag + "st"].h, st_v)
                        s.activation(rstd[:, sl4:sl4 + 1], rstd[:, sl4:sl4 + 1],
                                     AF.Sqrt).then_inc(sm[tag + "sq"].h, 1)
                    ops["s"].append(sqr)
                    sq_done[tt] = sm[tag + "sq"].inc()

                    def rcp(v, tt=tt, sl4=sl4):
                        v.wait_ge(sm[tag + "sq"].h, sq_done[tt])
                        v.reciprocal(rstd[:, sl4:sl4 + 1], rstd[:, sl4:sl4 + 1])
                        v.drain()
                        v.tensor_scalar(nmr[:, sl4:sl4 + 1],
                                        agg[:, sl4 * 2:sl4 * 2 + 1],
                                        rstd[:, sl4:sl4 + 1],
                                        self.cst[:, CM1:CM1 + 1],
                                        op0=ALU.mult, op1=ALU.mult
                                        ).then_inc(sm[tag + "st"].h, 1)
                    ops["v"].append(rcp)
                    rc_v = sm[tag + "st"].inc()

                    def apply(s, sp=sp, sub=sub, xs=xs, tt=tt, T=T, rc_v=rc_v,
                              sl4=sl4):
                        s.wait_ge(sm[tag + "st"].h, rc_v)
                        if T >= 2 and sub == 0:
                            s.wait_ge(sm[tag + "tr"].h, tr_done[(T - 2, 15)])
                        s.activation(
                            lnb[:, (sp * 4 + sub) * D:(sp * 4 + sub + 1) * D],
                            xb[:, xs * D:(xs + 1) * D], AF.Identity,
                            scale=rstd[:, sl4:sl4 + 1], bias=nmr[:, sl4:sl4 + 1]
                        ).then_inc(sm[tag + "ap"].h, 1)
                    ops["s"].append(apply)
                    ap_done[tt] = sm[tag + "ap"].inc()

                for dp in range(16):
                    def tr(t, T=T, sp=sp, dp=dp):
                        t.wait_ge(sm[tag + "ap"].h, ap_done[T * 4 + 3])
                        if T > 0 or dp >= 4:
                            pT, pdp = (T, dp - 4) if dp >= 4 else (T - 1, 12 + dp)
                            t.wait_ge(sm[tag + "ev%d" % (pdp % 2)].h,
                                      ev_done[(pT, pdp)])
                        for j in range(2):
                            dc = dp * 2 + j
                            for sub in range(4):
                                inst = t.transpose(
                                    pst[:, (dp % 4) * 1024 + j * 512 + sub * 128:
                                        (dp % 4) * 1024 + j * 512 + (sub + 1) * 128],
                                    lnb[:, (sp * 4 + sub) * D + dc * 128:
                                        (sp * 4 + sub) * D + (dc + 1) * 128],
                                    self.id_bf[:, :])
                        inst.then_inc(sm[tag + "tr"].h, 1)
                    ops["t"].append(tr)
                    tr_done[(T, dp)] = sm[tag + "tr"].inc()

                    def ev(e, T=T, dp=dp):
                        e.wait_ge(sm[tag + "tr"].h, tr_done[(T, dp)])
                        if T > 0 or dp >= 4:
                            pT, pdp = (T, dp - 4) if dp >= 4 else (T - 1, 12 + dp)
                            e.wait_ge(sm[tag + "dm%d" % (pdp % 4)].h,
                                      dm_done[(pT, pdp)])
                        if dp % 2 == 0:
                            e.activation(stg[:, (dp % 4) * 1024:(dp % 4 + 1) * 1024],
                                         pst[:, (dp % 4) * 1024:(dp % 4 + 1) * 1024],
                                         AF.Copy).then_inc(sm[tag + "ev0"].h, 1)
                        else:
                            e.tensor_copy(stg[:, (dp % 4) * 1024:(dp % 4 + 1) * 1024],
                                          pst[:, (dp % 4) * 1024:(dp % 4 + 1) * 1024]
                                          ).then_inc(sm[tag + "ev1"].h, 1)
                    (ops["s"] if dp % 2 == 0 else ops["v"]).append(ev)
                    ev_done[(T, dp)] = sm[tag + "ev%d" % (dp % 2)].inc()

                    def st(sy, T=T, dp=dp):
                        sy.wait_ge(sm[tag + "ev%d" % (dp % 2)].h, ev_done[(T, dp)])
                        if qtile_blocked:
                            dst = lnT_scr[T][:, dp * 1024:(dp + 1) * 1024]
                        else:
                            dst = lnT_scr[dp * 2:dp * 2 + 2, :,
                                          tok_base + T * 512:
                                          tok_base + (T + 1) * 512
                                          ].transpose([1, 0, 2])
                        sy.dma_start(
                            dst if qtile_blocked else dst,
                            stg[:, (dp % 4) * 1024:(dp % 4 + 1) * 1024].rearrange(
                                "p (a f) -> p a f", a=2)
                            if not qtile_blocked else
                            stg[:, (dp % 4) * 1024:(dp % 4 + 1) * 1024]
                        ).then_inc(sm[tag + "dm%d" % (dp % 4)].h, 16)
                    ops["y"].append(st)
                    dm_done[(T, dp)] = sm[tag + "dm%d" % (dp % 4)].inc(16)

            with nc.Block() as blk:
                blk.sync(lambda sy: [f(sy) for f in ops["y"]])
                blk.vector(lambda v: [f(v) for f in ops["v"]])
                blk.scalar(lambda s: [f(s) for f in ops["s"]])
                blk.tensor(lambda t: [f(t) for f in ops["t"]])

            with nc.Block() as blk:
                fv = [(sm[tag + "dm%d" % k].h, sm[tag + "dm%d" % k].n)
                      for k in range(4)]

                @blk.sync
                def _(sy):
                    for hh, vv in fv:
                        sy.wait_ge(hh, vv)

    def p1a_ln1(self):
        def src_of(tt):
            src = self.x_own if tt < 16 else self.x_oth
            r0 = (tt % 16) * 128
            return src[r0:r0 + 128, :]
        self._ln_transpose("l1", 32, src_of, self.s_ln1T, 0)

    def p2_q(self):
        nc = self.nc
        sm = self.sems(["q2g0", "q2g1", "q2g2", "q2g3", "q2ld", "q2mm", "q2ev", "q2dm0", "q2dm1"])
        with ExitStack() as ph:
            qln = self.sbuf(ph, "qln", [128, 32 * OWN], BF16)
            wqt = self.sbuf(ph, "wqt", [128, 4 * 256], BF16)
            qstg = self.sbuf(ph, "qstg", [128, 2 * 4096], BF16)
            psq = self.psum(ph, "psq", [128, 4096], F32)

            with nc.Block() as blk:
                @blk.sync
                def _(sy):
                    for i in range(4):
                        sy.dma_start(
                            qln[:, i * 8 * OWN:(i + 1) * 8 * OWN].rearrange(
                                "p (a f) -> p a f", a=8),
                            self.s_ln1T[i * 8:(i + 1) * 8, :, 0:OWN]
                            .transpose([1, 0, 2])
                        ).then_inc(sm["q2ld"].h, 16)
                    sy.wait_ge(sm["q2ld"].h, sm["q2ld"].inc(64))

            ops = {k: [] for k in "tsvyg"}
            wq_done, mm_done, ev_done, dm_done = {}, {}, {}, {}
            for fg in range(16):
                for dc in range(32):
                    i = fg * 32 + dc
                    ws = i % 4

                    def wld(g, fg=fg, dc=dc, ws=ws, i=i):
                        if i >= 4:
                            g.wait_ge(sm["q2mm"].h, mm_done[i - 4])
                        g.dma_start(wqt[:, ws * 256:(ws + 1) * 256],
                                    self.Wq[fg, dc]
                                    ).then_inc(sm["q2g%d" % ws].h, 16)
                    ops["g"].append(wld)
                    wq_done[i] = sm["q2g%d" % ws].inc(16)

                    def mm(t, fg=fg, dc=dc, ws=ws, i=i):
                        t.wait_ge(sm["q2g%d" % ws].h, wq_done[i])
                        if dc == 0 and fg > 0:
                            t.wait_ge(sm["q2ev"].h, ev_done[fg - 1])
                        first, last = dc == 0, dc == 31
                        for f2 in range(2):
                            for qt in range(4):
                                inst = t.matmul(
                                    psq[:, (f2 * 4 + qt) * 512:
                                        (f2 * 4 + qt + 1) * 512],
                                    wqt[:, ws * 256 + f2 * 128:
                                        ws * 256 + (f2 + 1) * 128],
                                    qln[:, dc * OWN + qt * 512:
                                        dc * OWN + (qt + 1) * 512],
                                    start=first, stop=last)
                        inst.then_inc(sm["q2mm"].h, 1)
                    ops["t"].append(mm)
                    mm_done[i] = sm["q2mm"].inc()

                def qev(v, fg=fg):
                    v.wait_ge(sm["q2mm"].h, mm_done[fg * 32 + 31])
                    if fg >= 2:
                        v.wait_ge(sm["q2dm%d" % (fg % 2)].h, dm_done[fg - 2])
                    inst = v.tensor_copy(
                        qstg[:, (fg % 2) * 4096:(fg % 2 + 1) * 4096], psq[:, :])
                    inst.then_inc(sm["q2ev"].h, 1)
                ops["v"].append(qev)
                ev_done[fg] = sm["q2ev"].inc()

                def qdm(sy, fg=fg):
                    sy.wait_ge(sm["q2ev"].h, ev_done[fg])
                    for qt in range(4):
                        sy.dma_start(
                            self.s_q[qt][:, fg * 2 * QTW:(fg * 2 + 2) * QTW]
                            .rearrange("p (a f) -> p a f", a=2),
                            qstg[:, (fg % 2) * 4096:(fg % 2 + 1) * 4096]
                            .rearrange("p (a f) -> p a f", a=2)[
                                :, :, qt * QTW:(qt + 1) * QTW]
                        ).then_inc(sm["q2dm%d" % (fg % 2)].h, 16)
                ops["y"].append(qdm)
                dm_done[fg] = sm["q2dm%d" % (fg % 2)].inc(64)

            with nc.Block() as blk:
                blk.gpsimd(lambda g: [f(g) for f in ops["g"]])
                blk.tensor(lambda t: [f(t) for f in ops["t"]])
                blk.vector(lambda v: [f(v) for f in ops["v"]])
                blk.sync(lambda sy: [f(sy) for f in ops["y"]])

            with nc.Block() as blk:
                f0, f1 = sm["q2dm0"].n, sm["q2dm1"].n

                @blk.sync
                def _(sy):
                    sy.wait_ge(sm["q2dm0"].h, f0)
                    sy.wait_ge(sm["q2dm1"].h, f1)

    def p1b_kv(self, kv_stack):
        nc = self.nc
        self.KT_sb = self.sbuf(kv_stack, "KT_sb", [128, NKV * S], BF16)
        self.V_sb = self.sbuf(kv_stack, "V_sb", [128, 32 * 512], BF16)
        sm = self.sems(["kvw", "kvl0", "kvl1", "kvm", "kvev", "vsm", "vse"])
        with ExitStack() as ph:
            Wk_sb = self.sbuf(ph, "Wk_sb", [128, 32 * 512], BF16)
            Wv_sb = self.sbuf(ph, "Wv_sb", [128, 32 * 512], BF16)
            lnt = self.sbuf(ph, "kvlnt", [128, 2 * 1024], BF16)
            psk = self.psum(ph, "psk", [128, 4096], F32)

            with nc.Block() as blk:
                @blk.gpsimd
                def _(g):
                    g.dma_start(Wk_sb[:, :], self.Wk).then_inc(sm["kvw"].h, 16)
                    g.dma_start(Wv_sb[:, :], self.Wv).then_inc(sm["kvw"].h, 16)
                    g.wait_ge(sm["kvw"].h, sm["kvw"].inc(2 * 16))

            ops = {k: [] for k in "tsvyg"}
            ld_done, mm_done, ev_done = {}, {}, {}
            for kind in ("K", "V"):
                for p in range(4):
                    for dc in range(32):
                        i = (0 if kind == "K" else 128) + p * 32 + dc
                        ws = i % 2

                        def lld(sy, p=p, dc=dc, ws=ws, i=i):
                            if i >= 2:
                                sy.wait_ge(sm["kvm"].h, mm_done[i - 2])
                            sy.dma_start(
                                lnt[:, ws * 1024:(ws + 1) * 1024],
                                self.s_ln1T[dc, :, p * 1024:(p + 1) * 1024]
                            ).then_inc(sm["kvl%d" % ws].h, 16)
                        ops["y"].append(lld)
                        ld_done[i] = sm["kvl%d" % ws].inc(16)

                        def kmm(t, kind=kind, p=p, dc=dc, ws=ws, i=i):
                            t.wait_ge(sm["kvl%d" % ws].h, ld_done[i])
                            j = i // 32
                            if dc == 0 and j >= 1:
                                t.wait_ge(sm["kvev"].h, ev_done[j - 1])
                            first, last = dc == 0, dc == 31
                            if kind == "K":
                                for g4 in range(4):
                                    for t2 in range(2):
                                        inst = t.matmul(
                                            psq_slice(psk, g4 * 2 + t2),
                                            Wk_sb[:, dc * 512 + g4 * 128:
                                                  dc * 512 + (g4 + 1) * 128],
                                            lnt[:, ws * 1024 + t2 * 512:
                                                ws * 1024 + (t2 + 1) * 512],
                                            start=first, stop=last)
                            else:
                                for t8 in range(8):
                                    inst = t.matmul(
                                        psq_slice(psk, t8),
                                        lnt[:, ws * 1024 + t8 * 128:
                                            ws * 1024 + (t8 + 1) * 128],
                                        Wv_sb[:, dc * 512:(dc + 1) * 512],
                                        start=first, stop=last)
                            inst.then_inc(sm["kvm"].h, 1)
                        ops["t"].append(kmm)
                        mm_done[i] = sm["kvm"].inc()

                    def kev(v, kind=kind, p=p):
                        i = (0 if kind == "K" else 128) + p * 32 + 31
                        v.wait_ge(sm["kvm"].h, mm_done[i])
                        if kind == "K":
                            for g4 in range(4):
                                for t2 in range(2):
                                    inst = v.tensor_copy(
                                        self.KT_sb[:, g4 * S + p * 1024 + t2 * 512:
                                                   g4 * S + p * 1024 + (t2 + 1) * 512],
                                        psq_slice(psk, g4 * 2 + t2))
                        else:
                            for t8 in range(8):
                                kt = p * 8 + t8
                                inst = v.tensor_copy(
                                    self.V_sb[:, kt * 512:(kt + 1) * 512],
                                    psq_slice(psk, t8))
                        inst.then_inc(sm["kvev"].h, 1)
                    ops["v"].append(kev)
                    ev_done[(0 if kind == "K" else 4) + p] = sm["kvev"].inc()


            with nc.Block() as blk:
                blk.sync(lambda sy: [f(sy) for f in ops["y"]])
                blk.tensor(lambda t: [f(t) for f in ops["t"]])
                blk.vector(lambda v: [f(v) for f in ops["v"]])

            with nc.Block() as blk:
                @blk.tensor
                def _(t):
                    for g4 in range(4):
                        for kt in range(32):
                            inst = t.matmul(psq_slice(psk, g4)[:, 0:1],
                                            self.V_sb[:, kt * 512 + g4 * 128:
                                                      kt * 512 + (g4 + 1) * 128],
                                            self.onescol[:, :],
                                            start=(kt == 0), stop=(kt == 31))
                        inst.then_inc(sm["vsm"].h, 1)

                @blk.vector
                def _(v):
                    v.wait_ge(sm["vsm"].h, sm["vsm"].inc(4))
                    for g4 in range(4):
                        v.tensor_copy(self.Vsum_bf[:, g4:g4 + 1],
                                      psq_slice(psk, g4)[:, 0:1])

    def p3_attention(self):
        nc = self.nc
        sm = self.sems(["a3sc", "a3ex", "a3mk", "a3av", "a3evc", "a3sd0",
                        "a3sd1", "a3rc", "a3rd0", "a3rd1", "a3rd2", "a3rd3",
                        "a3rd4", "a3rd5",
                        "a3rw", "a3nm", "a3qt", "a3od", "a3oe"])
        with ExitStack() as ph:
            QT_sb = self.sbuf(ph, "QT_sb", [128, NQ * QTW], BF16)
            PT = self.sbuf(ph, "PT", [128, 3 * 1024], BF16)
            OTQ = self.sbuf(ph, "OTQ", [128, 2 * NQ * QTW], BF16)
            srow = self.sbuf(ph, "srow", [1, 2 * QTW], F32)
            sums_all = self.sbuf(ph, "sums_all", [32, QTW], F32)
            rec_all = self.sbuf(ph, "rec_all", [32, QTW], F32)
            rbc = self.sbuf(ph, "rbc", [128, 6 * QTW], F32)
            ps_sc = self.psum(ph, "ps_sc", [128, 3072], F32)
            ps_av = self.psum(ph, "ps_av", [128, QTW], F32)
            ps_sum = self.psum(ph, "ps_sum", [1, QTW], F32)

            base = {k: v.n for k, v in sm.items()}
            finals = []

            for body in nc.Switch(engines=self.sw_engines,
                                  index=[self.parity] * 5, n=2):
                for k in sm:
                    sm[k].n = base[k]
                half = body
                ownq = QTILES[half]
                othq = QTILES[1 - half]

                def gblock(pb):
                    if pb < 16:
                        return ownq[pb // 4] * 4 + pb % 4
                    return othq[(pb - 16) // 4] * 4 + pb % 4

                exp_done, msk_done, avq_done, sc_done = {}, {}, {}, {}
                evac_done, sdma_done, nrm_done = {}, {}, {}
                fq = 0
                qt_load_done = {}
                otd_done = {}

                for lt in range(4):
                    gq = ownq[lt]
                    ob = lt % 2 * NQ * QTW
                    full = [pb for pb in range(32) if gblock(pb) >= 4 * gq + 4]
                    partial = [lt * 4, lt * 4 + 1, lt * 4 + 2, lt * 4 + 3]
                    kts = full + partial
                    nq2 = len(kts) // 2
                    quads = [(kts[2 * i], kts[2 * i + 1]) for i in range(nq2)]
                    npart = 2
                    fixq = (gq == 7)

                    if lt == 0:
                        def qtl0(g):
                            g.dma_start(QT_sb[:, :], self.s_q[0]
                                        ).then_inc(sm["a3qt"].h, 16)
                        self._emit(nc.gpsimd, qtl0)
                        qt_load_done[0] = sm["a3qt"].inc(16)

                    for h in range(NQ):
                        u = lt * 32 + h
                        g4 = h // GRP
                        uq0 = fq
                        for qi, (ka, kb) in enumerate(quads):
                            p = fq % 3
                            is_part = qi >= nq2 - npart
                            first, last = qi == 0, qi == nq2 - 1

                            def pe_sc(t, ka=ka, kb=kb, p=p, fq=fq, g4=g4,
                                      lt=lt, u=u, qi=qi):
                                if qi == 0:
                                    t.wait_ge(sm["a3qt"].h, qt_load_done[lt])
                                if fq >= 3:
                                    t.wait_ge(sm["a3ex"].h, exp_done[fq - 3])
                                for i, kt in enumerate((ka, kb)):
                                    inst = t.matmul(
                                        ps_sc[:, p * 1024 + i * 512:
                                              p * 1024 + (i + 1) * 512],
                                        self.KT_sb[:, g4 * S + kt * 128:
                                                   g4 * S + (kt + 1) * 128],
                                        QT_sb[:, h * QTW:(h + 1) * QTW],
                                        start=True, stop=True)
                                inst.then_inc(sm["a3sc"].h, 1)
                            sc_done[fq] = sm["a3sc"].inc()

                            def act_ex(s, p=p, fq=fq):
                                s.wait_ge(sm["a3sc"].h, sc_done[fq])
                                if fq >= 3:
                                    s.wait_ge(sm["a3av"].h, avq_done[fq - 3])
                                s.activation(PT[:, p * 1024:(p + 1) * 1024],
                                             ps_sc[:, p * 1024:(p + 1) * 1024],
                                             AF.Exp, scale=ASCALE
                                             ).then_inc(sm["a3ex"].h, 1)
                            exp_done[fq] = sm["a3ex"].inc()

                            if is_part:
                                mp = 0 if qi == nq2 - 2 else 1

                                def dve_mk(v, p=p, fq=fq, mp=mp):
                                    v.wait_ge(sm["a3ex"].h, exp_done[fq])
                                    v.tensor_tensor(
                                        PT[:, p * 1024:(p + 1) * 1024],
                                        PT[:, p * 1024:(p + 1) * 1024],
                                        self.maskp_bf[:, mp * 1024:(mp + 1) * 1024],
                                        op=ALU.mult).then_inc(sm["a3mk"].h, 1)
                                msk_done[fq] = sm["a3mk"].inc()

                            def pe_av(t, ka=ka, kb=kb, p=p, fq=fq, g4=g4, u=u,
                                      first=first, last=last, is_part=is_part):
                                if is_part:
                                    t.wait_ge(sm["a3mk"].h, msk_done[fq])
                                else:
                                    t.wait_ge(sm["a3ex"].h, exp_done[fq])
                                if first and u >= 1:
                                    t.wait_ge(sm["a3evc"].h, evac_done[u - 1])
                                for i, kt in enumerate((ka, kb)):
                                    t.matmul(ps_av[:, :],
                                             self.V_sb[:, kt * 512 + g4 * 128:
                                                       kt * 512 + (g4 + 1) * 128],
                                             PT[:, p * 1024 + i * 512:
                                                p * 1024 + (i + 1) * 512],
                                             start=(first and i == 0),
                                             stop=(last and i == 1))
                                for i in range(2):
                                    inst = t.matmul(
                                        ps_sum[:, :], self.onescol[:, :],
                                        PT[:, p * 1024 + i * 512:
                                           p * 1024 + (i + 1) * 512],
                                        start=(first and i == 0),
                                        stop=(last and i == 1))
                                inst.then_inc(sm["a3av"].h, 1)
                            avq_done[fq] = sm["a3av"].inc()

                            if qi == 0:
                                self._emit(nc.tensor, pe_sc)
                                self._p3_pend = []
                            elif qi <= 2:
                                self._emit(nc.tensor, pe_sc)
                            self._p3_pend.append((pe_av, pe_sc, qi))
                            self._emit(nc.scalar, act_ex)
                            if is_part:
                                self._emit(nc.vector, dve_mk)
                            fq += 1

                        for qi, (av_f, sc_f, _q) in enumerate(self._p3_pend):
                            self._emit(nc.tensor, av_f)
                            if qi + 3 < nq2:
                                self._emit(nc.tensor,
                                           self._p3_pend[qi + 3][1])

                        def dve_ev(v, u=u, h=h, g4=g4, fixq=fixq, lt=lt, ob=ob):
                            v.wait_ge(sm["a3av"].h, avq_done[uq0 + nq2 - 1])
                            if u >= 2:
                                v.wait_ge(sm["a3sd%d" % (u % 2)].h,
                                          sdma_done[u - 2])
                            if lt > 1 and u % 32 == 0:
                                v.wait_ge(sm["a3o%s" % "de"[lt % 2]].h,
                                          otd_done[lt - 2])
                            v.tensor_copy(OTQ[:, ob + h * QTW:ob + (h + 1) * QTW],
                                          ps_av[:, :])
                            if fixq:
                                v.drain()
                                v.tensor_copy(
                                    OTQ[:, ob + h * QTW + 511:ob + h * QTW + 512],
                                    self.Vsum_bf[:, g4:g4 + 1])
                            inst = v.tensor_copy(
                                srow[:, (u % 2) * QTW:(u % 2 + 1) * QTW],
                                ps_sum[:, :])
                            inst.then_inc(sm["a3evc"].h, 1)
                        self._emit(nc.vector, dve_ev)
                        evac_done[u] = sm["a3evc"].inc()

                        def sy_sr(sy, u=u, h=h):
                            sy.wait_ge(sm["a3evc"].h, evac_done[u])
                            sy.dma_start(sums_all[h:h + 1, :],
                                         srow[:, (u % 2) * QTW:(u % 2 + 1) * QTW]
                                         ).then_inc(sm["a3sd%d" % (u % 2)].h, 16)
                        self._emit(nc.scalar, sy_sr)
                        sdma_done[u] = sm["a3sd%d" % (u % 2)].inc(16)

                    if lt < 3:
                        def qtl(g, lt=lt, scv=sm["a3sc"].n):
                            g.wait_ge(sm["a3sc"].h, scv)
                            g.dma_start(QT_sb[:, :], self.s_q[lt + 1]
                                        ).then_inc(sm["a3qt"].h, 16)
                        self._emit(nc.gpsimd, qtl)
                        qt_load_done[lt + 1] = sm["a3qt"].inc(16)

                    def dve_rc(v, lt=lt, fixq=fixq, u_last=lt * 32 + 31,
                               rwv=sm["a3rw"].n):
                        v.wait_ge(sm["a3rw"].h, rwv)
                        v.wait_ge(sm["a3sd0"].h, sdma_done[u_last - 1])
                        v.wait_ge(sm["a3sd1"].h, sdma_done[u_last])
                        if fixq:
                            v.memset(sums_all[:, 511:512], 4096.0)
                            v.drain()
                        v.reciprocal(rec_all[:, :], sums_all[:, :]
                                     ).then_inc(sm["a3rc"].h, 1)
                    self._emit(nc.vector, dve_rc)
                    rc_done = sm["a3rc"].inc()

                    def sy_rw(sy, rc_done=rc_done,
                              rds=tuple((("a3rd%d" % k), sm["a3rd%d" % k].n)
                                        for k in range(6))):
                        sy.wait_ge(sm["a3rc"].h, rc_done)
                        for nm_, vv in rds:
                            sy.wait_ge(sm[nm_].h, vv)
                        sy.dma_start(self.s_row32, rec_all[:, :]
                                     ).then_inc(sm["a3rw"].h, 16)
                    self._emit(nc.sync, sy_rw)
                    rw_done = sm["a3rw"].inc(16)

                    for h in range(NQ):
                        u = lt * 32 + h

                        def sy_rr(sy, u=u, h=h, rw_done=rw_done):
                            if h == 0:
                                sy.wait_ge(sm["a3rw"].h, rw_done)
                            if h >= 6:
                                sy.wait_ge(sm["a3nm"].h, nrm_done[u - 6])
                            sy.dma_start(
                                rbc[:, (h % 6) * QTW:(h % 6 + 1) * QTW],
                                self.s_row32[h:h + 1, :]
                                .to_broadcast([128, QTW])
                            ).then_inc(sm["a3rd%d" % (h % 6)].h, 16)
                        self._emit(nc.sync, sy_rr)
                        rd_done = sm["a3rd%d" % (h % 6)].inc(16)

                        def gp_nm(g, u=u, h=h, rd_done=rd_done, ob=ob):
                            g.wait_ge(sm["a3rd%d" % (h % 6)].h, rd_done)
                            g.tensor_tensor(OTQ[:, ob + h * QTW:ob + (h + 1) * QTW],
                                            OTQ[:, ob + h * QTW:ob + (h + 1) * QTW],
                                            rbc[:, (h % 6) * QTW:(h % 6 + 1) * QTW],
                                            op=ALU.mult).then_inc(sm["a3nm"].h, 1)
                        self._emit(nc.gpsimd, gp_nm)
                        nrm_done[u] = sm["a3nm"].inc()

                        def sy_ot(sy, u=u, h=h, lt=lt, ob=ob):
                            sy.wait_ge(sm["a3nm"].h, nrm_done[u])
                            for mtl in range(4):
                                sy.dma_start(
                                    self.s_ot[lt * 4 + mtl, :,
                                              h * 128:(h + 1) * 128],
                                    OTQ[:, ob + h * QTW + mtl * 128:
                                        ob + h * QTW + (mtl + 1) * 128]
                                ).then_inc(sm["a3o%s" % "de"[lt % 2]].h, 16)
                        self._emit(nc.sync, sy_ot)
                        sm["a3o%s" % "de"[lt % 2]].inc(64)
                    otd_done[lt] = sm["a3o%s" % "de"[lt % 2]].n

                finals.append({k: v.n for k, v in sm.items()})

                def fin_wait(sy):
                    sy.wait_ge(sm["a3od"].h, sm["a3od"].n)
                    sy.wait_ge(sm["a3oe"].h, sm["a3oe"].n)
                    for k in range(6):
                        sy.wait_ge(sm["a3rd%d" % k].h, sm["a3rd%d" % k].n)
                self._emit(nc.sync, fin_wait)

            assert finals[0] == finals[1], (finals[0], finals[1])
            for k in sm:
                sm[k].n = finals[0][k]
            self._ot_finals = (sm["a3od"].h, sm["a3od"].n,
                               sm["a3oe"].h, sm["a3oe"].n)

        with nc.Block() as blk:
            @blk.sync
            def _(sy):
                pass

    def _emit(self, eng, f):
        f(eng)

    def p4_wo(self):
        nc = self.nc
        NPASS = 4
        PW = D // NPASS
        sm = self.sems(["w4ga", "w4gb", "w4o0", "w4o1", "w4x0", "w4x1", "w4m",
                        "w4e", "w4d0", "w4d1"])
        with ExitStack() as ph:
            Wo_sb = self.sbuf(ph, "Wo_sb", [128, 2 * 32 * PW], BF16)
            otm = self.sbuf(ph, "otm", [128, 2 * 4096], BF16)
            xtl = self.sbuf(ph, "xtl", [128, 2 * PW], F32)
            x2t = self.sbuf(ph, "x2t", [128, 2 * PW], F32)
            psw = self.psum(ph, "psw", [128, 2 * PW], F32)

            ops = {k: [] for k in "tsvyg"}
            ot_done, x_done, mm_done, ev_done, dm_done = {}, {}, {}, {}, {}
            wo_done = {}

            for np_ in range(NPASS):
                wsl = np_ % 2

                def wol(g, np_=np_, wsl=wsl):
                    if np_ >= 2:
                        g.wait_ge(sm["w4m"].h, mm_done[(np_ - 2, NMT - 1)])
                    g.dma_start(Wo_sb[:, wsl * 32 * PW:(wsl + 1) * 32 * PW],
                                self.Wo[np_]
                                ).then_inc(sm["w4g%s" % "ab"[wsl]].h, 16)
                ops["g"].append(wol)
                wo_done[np_] = sm["w4g%s" % "ab"[wsl]].inc(16)

                for mt in range(NMT):
                    i = np_ * NMT + mt
                    ws = i % 2

                    def otl(sy, mt=mt, ws=ws, i=i):
                        if i >= 2:
                            sy.wait_ge(sm["w4m"].h, mm_done[divmod(i - 2, NMT)])
                        sy.dma_start(otm[:, ws * 4096:(ws + 1) * 4096],
                                     self.s_ot[mt, :, :]
                                     ).then_inc(sm["w4o%d" % ws].h, 16)
                    ops["y"].append(otl)
                    ot_done[i] = sm["w4o%d" % ws].inc(16)

                    def xl(sy, np_=np_, mt=mt, ws=ws, i=i):
                        if i >= 2:
                            sy.wait_ge(sm["w4e"].h, ev_done[divmod(i - 2, NMT)])
                        sy.dma_start(xtl[:, ws * PW:(ws + 1) * PW],
                                     self.x_own[mt * 128:(mt + 1) * 128,
                                                np_ * PW:(np_ + 1) * PW]
                                     ).then_inc(sm["w4x%d" % ws].h, 16)
                    ops["y"].append(xl)
                    x_done[i] = sm["w4x%d" % ws].inc(16)

                    def wmm(t, np_=np_, mt=mt, ws=ws, i=i):
                        wsl = np_ % 2
                        t.wait_ge(sm["w4o%d" % ws].h, ot_done[i])
                        if mt == 0:
                            t.wait_ge(sm["w4g%s" % "ab"[wsl]].h, wo_done[np_])
                        if i >= 2:
                            t.wait_ge(sm["w4e"].h, ev_done[divmod(i - 2, NMT)])
                        for fc in range(32):
                            for nt in range(PW // 512):
                                inst = t.matmul(
                                    psw[:, ws * PW + nt * 512:
                                        ws * PW + (nt + 1) * 512],
                                    otm[:, ws * 4096 + fc * 128:
                                        ws * 4096 + (fc + 1) * 128],
                                    Wo_sb[:, (wsl * 32 + fc) * PW + nt * 512:
                                          (wsl * 32 + fc) * PW + (nt + 1) * 512],
                                    start=(fc == 0), stop=(fc == 31))
                        inst.then_inc(sm["w4m"].h, 1)
                    ops["t"].append(wmm)
                    mm_done[(np_, mt)] = sm["w4m"].inc()

                    def wev(v, np_=np_, mt=mt, ws=ws, i=i):
                        v.wait_ge(sm["w4m"].h, mm_done[(np_, mt)])
                        v.wait_ge(sm["w4x%d" % ws].h, x_done[i])
                        if i >= 2:
                            v.wait_ge(sm["w4d%d" % ws].h, dm_done[i - 2])
                        v.tensor_tensor(x2t[:, ws * PW:(ws + 1) * PW],
                                        psw[:, ws * PW:(ws + 1) * PW],
                                        xtl[:, ws * PW:(ws + 1) * PW],
                                        op=ALU.add).then_inc(sm["w4e"].h, 1)
                    ops["v"].append(wev)
                    ev_done[(np_, mt)] = sm["w4e"].inc()

                    def wdm(sy, np_=np_, mt=mt, ws=ws, i=i):
                        sy.wait_ge(sm["w4e"].h, ev_done[(np_, mt)])
                        sy.dma_start(self.s_x2[mt * 128:(mt + 1) * 128,
                                               np_ * PW:(np_ + 1) * PW],
                                     x2t[:, ws * PW:(ws + 1) * PW]
                                     ).then_inc(sm["w4d%d" % ws].h, 16)
                    ops["y"].append(wdm)
                    dm_done[i] = sm["w4d%d" % ws].inc(16)

            with nc.Block() as blk:
                blk.gpsimd(lambda g: [f(g) for f in ops["g"]])
                blk.sync(lambda sy: [f(sy) for f in ops["y"]])
                blk.tensor(lambda t: [f(t) for f in ops["t"]])
                blk.vector(lambda v: [f(v) for f in ops["v"]])

            with nc.Block() as blk:
                f0, f1 = sm["w4d0"].n, sm["w4d1"].n

                @blk.sync
                def _(sy):
                    sy.wait_ge(sm["w4d0"].h, f0)
                    sy.wait_ge(sm["w4d1"].h, f1)

    def p4c_ln2(self):
        def src_of(tt):
            return self.s_x2[tt * 128:(tt + 1) * 128, :]

        bu = self

        def save_stats(v, tt, agg):
            v.tensor_copy(bu.mean2_sb[:, tt:tt + 1], agg[:, 0:1])
            v.drain()
            v.tensor_copy(bu.rstd2_sb[:, tt:tt + 1], agg[:, 1:2])
            v.drain()

        self._ln_transpose("l2", NMT, src_of, self.s_ln2T, 0,
                           save_stats=save_stats, qtile_blocked=True)

        nc = self.nc
        sm = self.sems(["c2a", "c2b"])
        with ExitStack() as ph:
            tmp = self.sbuf(ph, "c2tmp", [128, NMT], F32)
            with nc.Block() as blk:
                @blk.vector
                def _(v):
                    v.tensor_scalar(tmp[:, :], self.rstd2_sb[:, :],
                                    self.cst[:, CEPS:CEPS + 1], None, op0=ALU.add
                                    ).then_inc(sm["c2a"].h, 1)

                @blk.scalar
                def _(s):
                    s.wait_ge(sm["c2a"].h, sm["c2a"].inc(1))
                    s.activation(tmp[:, :], tmp[:, :], AF.Sqrt
                                 ).then_inc(sm["c2b"].h, 1)

            with nc.Block() as blk:
                @blk.vector
                def _(v):
                    v.reciprocal(tmp[:, :], tmp[:, :])
                    v.drain()
                    v.tensor_tensor(self.logv_sb[:, :], tmp[:, :], tmp[:, :],
                                    op=ALU.mult)
                    v.drain()
                    v.tensor_tensor(self.logv_sb[:, :], self.logv_sb[:, :],
                                    self.rstd2_sb[:, :], op=ALU.mult)
                    v.drain()
                    v.tensor_copy(self.rstd2_sb[:, :], tmp[:, :]
                                  ).then_inc(sm["c2a"].h, 1)

                @blk.scalar
                def _(s):
                    s.wait_ge(sm["c2a"].h, sm["c2a"].inc(1))
                    s.activation(self.logv_sb[:, :], self.logv_sb[:, :],
                                 AF.Ln, bias=1.0)

    def p5_ffn(self):
        nc = self.nc
        sm = self.sems(["f5l", "f5g", "f5a", "f5d",
                        "f5w0", "f5w1", "f5w2", "f5zd", "f5ld", "f5sel", "f5tr",
                        "f5am", "f5ae", "f5zm", "f5ze", "f5zp", "f5zv",
                        "f5hm", "f5hb", "f5lm", "f5le"])
        with ExitStack() as ph:
            auxv_sb = self.sbuf(ph, "auxv_sb", [128, 8 * HID], BF16)
            WvdT_sb = self.sbuf(ph, "WvdT_sb", [128, 32 * VB], BF16)
            WvuT_sb = self.sbuf(ph, "WvuT_sb", [128, 2 * HID], BF16)
            l2t = self.sbuf(ph, "l2t", [128, 32 * 512], BF16)
            qT_bf = self.sbuf(ph, "qT_bf", [BOT, 512], BF16)
            rq_bf = self.sbuf(ph, "rq_bf", [RDIM, 512], BF16)
            R_sb = self.sbuf(ph, "R_sb", [128, NSLOT], F32)
            A_sb = self.sbuf(ph, "A_sb", [128, NSLOT], F32)
            Rz_sb = self.sbuf(ph, "Rz_sb", [128, NSLOT], F32)
            T8 = self.sbuf(ph, "T8", [128, 8], F32)
            Msel = self.sbuf(ph, "Msel", [128, NSLOT], U8)
            Am_sb = self.sbuf(ph, "Am_sb", [128, NSLOT], F32)
            nmx = self.sbuf(ph, "nmx", [128, 1], F32)
            den = self.sbuf(ph, "den", [128, 1], F32)
            E_bf = self.sbuf(ph, "E_bf", [128, NSLOT], BF16)
            wT_q = self.sbuf(ph, "wT_q", [128, 8 * 512], BF16)
            ao_bf = self.sbuf(ph, "ao_bf", [128, 2 * 512], BF16)
            z1_bf = self.sbuf(ph, "z1_bf", [128, 1024], BF16)
            z2_bf = self.sbuf(ph, "z2_bf", [128, 4096], BF16)
            wu1t = self.sbuf(ph, "wu1t", [128, 3 * 1024], BF16)
            h1_bf = self.sbuf(ph, "h1_bf", [128, 4096], BF16)
            lrow = self.sbuf(ph, "lrow", [1, 512], F32)

            with nc.Block() as blk:
                @blk.gpsimd
                def _(g):
                    g.dma_start(auxv_sb[:, :], self.auxv
                                ).then_inc(sm["f5g"].h, 16)
                    g.dma_start(WvdT_sb[:, :], self.WvdT
                                ).then_inc(sm["f5g"].h, 16)
                    g.dma_start(WvuT_sb[:, :], self.WvuT
                                ).then_inc(sm["f5g"].h, 16)
                    g.wait_ge(sm["f5g"].h, sm["f5g"].inc(3 * 16))

            for qt in range(4):
                self._p5_qtile(qt, sm, auxv_sb, WvdT_sb, WvuT_sb, l2t, qT_bf,
                               rq_bf, R_sb, A_sb, Rz_sb, T8, Msel, Am_sb, nmx,
                               den, E_bf, wT_q, ao_bf, z1_bf, z2_bf, wu1t,
                               h1_bf, lrow)

            with nc.Block() as blk:
                fz, fl = sm["f5zd"].n, sm["f5ld"].n

                @blk.sync
                def _(sy):
                    sy.wait_ge(sm["f5zd"].h, fz)
                    sy.wait_ge(sm["f5ld"].h, fl)

    def _p5_qtile(self, qt, sm, auxv_sb, WvdT_sb, WvuT_sb, l2t, qT_bf, rq_bf,
                  R_sb, A_sb, Rz_sb, T8, Msel, Am_sb, nmx, den, E_bf, wT_q,
                  ao_bf, z1_bf, z2_bf, wu1t, h1_bf, lrow):
        nc = self.nc

        with ExitStack() as ps:
            ps_q = self.psum(ps, "p5q_%d" % qt, [BOT, 512])
            ps_r = self.psum(ps, "p5r_%d" % qt, [128, 2048])
            ps_w = self.psum(ps, "p5w_%d" % qt, [128, 1024], BF16)

            with nc.Block() as blk:
                @blk.sync
                def _(sy):
                    sy.dma_start(l2t[:, :], self.s_ln2T[qt]
                                 ).then_inc(sm["f5l"].h, 16)
                    sy.wait_ge(sm["f5l"].h, sm["f5l"].n + 16)
                sm["f5l"].inc(16)

            with nc.Block() as blk:
                @blk.tensor
                def _(t):
                    for dc in range(32):
                        inst = t.matmul(ps_q[:, :],
                                        self.WqFT_sb[:, dc * BOT:(dc + 1) * BOT],
                                        l2t[:, dc * 512:(dc + 1) * 512],
                                        start=(dc == 0), stop=(dc == 31))
                    inst.then_inc(sm["f5a"].h, 1)

                @blk.vector
                def _(v):
                    v.wait_ge(sm["f5a"].h, sm["f5a"].inc(1))
                    v.tensor_copy(qT_bf[:, :], ps_q[:, :])

            with nc.Block() as blk:
                @blk.tensor
                def _(t):
                    t.matmul(ps_q[0:RDIM, :], self.WrT_sb[:, :], qT_bf[:, :],
                             start=True, stop=True).then_inc(sm["f5a"].h, 1)

                @blk.vector
                def _(v):
                    v.wait_ge(sm["f5a"].h, sm["f5a"].inc(1))
                    v.tensor_copy(rq_bf[:, :], ps_q[0:RDIM, :])

            for tt in range(4):
                with nc.Block() as blk:
                    @blk.tensor
                    def _(t, tt=tt):
                        for i in range(2):
                            t.matmul(ps_r[:, i * 512:(i + 1) * 512],
                                     rq_bf[:, tt * 128:(tt + 1) * 128],
                                     self.rkT_sb[:, i * 512:(i + 1) * 512],
                                     start=True, stop=True)
                        for i in range(2):
                            inst = t.matmul(ps_r[:, 1024 + i * 512:
                                            1024 + (i + 1) * 512],
                                            qT_bf[:, tt * 128:(tt + 1) * 128],
                                            self.auxkT_sb[:, i * 512:(i + 1) * 512],
                                            start=True, stop=True)
                        inst.then_inc(sm["f5a"].h, 1)

                    @blk.vector
                    def _(v, tt=tt):
                        v.wait_ge(sm["f5a"].h, sm["f5a"].inc(1))
                        v.scalar_tensor_tensor(R_sb[:, :], ps_r[:, 0:1024],
                                               RSCALE, self.logrelbc[:, :],
                                               op0=ALU.mult, op1=ALU.add)
                        v.scalar_tensor_tensor(A_sb[:, :], ps_r[:, 1024:2048],
                                               FSCALE, self.logrelbc[:, :],
                                               op0=ALU.mult, op1=ALU.add)
                        v.drain()
                        v.max(out=T8[:, :], in_=R_sb[:, :])
                        v.drain()
                        v.match_replace(out=Rz_sb[:, :], in_to_replace=T8[:, :],
                                        in_values=R_sb[:, :], imm_value=-1e30)
                        v.drain()
                        v.tensor_tensor(Msel[:, :], Rz_sb[:, :], R_sb[:, :],
                                        op=ALU.is_lt)
                        v.drain()
                        v.select(Am_sb[:, :], Msel[:, :], A_sb[:, :],
                                 self.negc[:, :], add_drain=True)
                        v.drain()
                        v.tensor_reduce(nmx[:, :], Am_sb[:, :], axis=AX.X,
                                        op=ALU.max, negate=True
                                        ).then_inc(sm["f5sel"].h, 1)

                    @blk.scalar
                    def _(s, tt=tt):
                        s.wait_ge(sm["f5sel"].h, sm["f5sel"].inc(1))
                        s.activation(E_bf[:, :], Am_sb[:, :], AF.Exp,
                                     scale=1.0, bias=nmx[:, :],
                                     accum_out=den[:, :])

                with nc.Block() as blk:
                    @blk.vector
                    def _(v, tt=tt):
                        v.reciprocal(den[:, :], den[:, :])
                        v.drain()
                        v.tensor_scalar(E_bf[:, :], E_bf[:, :], den[:, :], None,
                                        op0=ALU.mult).then_inc(sm["f5d"].h, 1)

                    @blk.tensor
                    def _(t, tt=tt):
                        t.wait_ge(sm["f5d"].h, sm["f5d"].n + 1)
                        for sc in range(8):
                            inst = t.transpose(
                                ps_w[:, sc * 128:(sc + 1) * 128],
                                E_bf[:, sc * 128:(sc + 1) * 128],
                                self.id_bf[:, :])
                        inst.then_inc(sm["f5tr"].h, 1)
                    sm["f5d"].inc(1)

                    @blk.vector
                    def _(v, tt=tt):
                        v.wait_ge(sm["f5tr"].h, sm["f5tr"].inc(1))
                        for sc in range(8):
                            v.tensor_copy(
                                wT_q[:, sc * 512 + tt * 128:
                                     sc * 512 + (tt + 1) * 128],
                                ps_w[:, sc * 128:(sc + 1) * 128])

        self._p5_aux_z1(qt, sm, auxv_sb, WvdT_sb, wT_q, ao_bf, z1_bf)

        self._p5_z2(qt, sm, z1_bf, WvuT_sb, z2_bf)

        self._p5_h1(qt, sm, l2t, wu1t, h1_bf, lrow)

    def _p5_aux_z1(self, qt, sm, auxv_sb, WvdT_sb, wT_q, ao_bf, z1_bf):
        nc = self.nc
        with ExitStack() as ps:
            ps_ao = self.psum(ps, "p5ao_%d" % qt, [128, 1024])
            ps_z1 = self.psum(ps, "p5z1_%d" % qt, [128, 1024])
            am_done, ae_done = {}, {}
            pe_ops, dv_ops = [], []
            for dc in range(32):
                sl = dc % 2

                def pe_aux(t, dc=dc, sl=sl):
                    if dc >= 2:
                        t.wait_ge(sm["f5ae"].h, ae_done[dc - 2])
                    for sc in range(8):
                        inst = t.matmul(
                            ps_ao[:, sl * 512:(sl + 1) * 512],
                            auxv_sb[:, sc * HID + dc * 128:
                                    sc * HID + (dc + 1) * 128],
                            wT_q[:, sc * 512:(sc + 1) * 512],
                            start=(sc == 0), stop=(sc == 7))
                    inst.then_inc(sm["f5am"].h, 1)
                am_done[dc] = sm["f5am"].inc()

                def dv_ev(v, dc=dc, sl=sl):
                    v.wait_ge(sm["f5am"].h, am_done[dc])
                    v.tensor_copy(ao_bf[:, sl * 512:(sl + 1) * 512],
                                  ps_ao[:, sl * 512:(sl + 1) * 512]
                                  ).then_inc(sm["f5ae"].h, 1)
                dv_ops.append(dv_ev)
                ae_done[dc] = sm["f5ae"].inc()

                def pe_z1(t, dc=dc, sl=sl):
                    t.wait_ge(sm["f5ae"].h, ae_done[dc])
                    for m in range(2):
                        inst = t.matmul(
                            ps_z1[:, m * 512:(m + 1) * 512],
                            WvdT_sb[:, dc * VB + m * 128:dc * VB + (m + 1) * 128],
                            ao_bf[:, sl * 512:(sl + 1) * 512],
                            start=(dc == 0), stop=(dc == 31))
                    if dc == 31:
                        inst.then_inc(sm["f5zm"].h, 1)
                pe_ops.append((pe_aux, pe_z1))
            zm_done = sm["f5zm"].inc()

            def dv_z1(v):
                v.wait_ge(sm["f5zm"].h, zm_done)
                v.tensor_copy(z1_bf[:, :], ps_z1[:, :]).then_inc(sm["f5ze"].h, 1)
            ze_done = sm["f5ze"].inc()

            with nc.Block() as blk:
                def pe_all(t):
                    for dc, (pa, pz) in enumerate(pe_ops):
                        pa(t)
                        if dc >= 1:
                            pe_ops[dc - 1][1](t)
                    pe_ops[31][1](t)
                blk.tensor(pe_all)

                def dv_all(v):
                    for f in dv_ops:
                        f(v)
                    dv_z1(v)
                blk.vector(dv_all)

    def _p5_z2(self, qt, sm, z1_bf, WvuT_sb, z2_bf):
        nc = self.nc
        with ExitStack() as ps:
            ps_z2 = self.psum(ps, "p5z2_%d" % qt, [128, 4096])
            base_zd = sm["f5zd"].n
            mm_done = {tt: sm["f5zp"].inc() for tt in range(4)}
            ev_done = {tt: sm["f5zv"].inc() for tt in range(4)}
            dm_done = {tt: sm["f5zd"].inc(16) for tt in range(4)}
            with nc.Block() as blk:
                def pe_all(t):
                    for tt in range(4):
                        if tt > 0:
                            t.wait_ge(sm["f5zv"].h, ev_done[tt - 1])
                        for m in range(2):
                            for nt in range(8):
                                inst = t.matmul(
                                    ps_z2[:, nt * 512:(nt + 1) * 512],
                                    z1_bf[:, m * 512 + tt * 128:
                                          m * 512 + (tt + 1) * 128],
                                    WvuT_sb[:, m * HID + nt * 512:
                                            m * HID + (nt + 1) * 512],
                                    start=(m == 0), stop=(m == 1))
                        inst.then_inc(sm["f5zp"].h, 1)
                blk.tensor(pe_all)

                def dv_all(v):
                    for tt in range(4):
                        v.wait_ge(sm["f5zp"].h, mm_done[tt])
                        if tt >= 1:
                            v.wait_ge(sm["f5zd"].h, dm_done[tt - 1])
                        else:
                            v.wait_ge(sm["f5zd"].h, base_zd)
                        v.tensor_copy(z2_bf[:, :], ps_z2[:, :]
                                      ).then_inc(sm["f5zv"].h, 1)
                blk.vector(dv_all)

                def sy_all(sy):
                    for tt in range(4):
                        sy.wait_ge(sm["f5zv"].h, ev_done[tt])
                        r0 = qt * 512 + tt * 128
                        sy.dma_start(self.s_z2[r0:r0 + 128, :], z2_bf[:, :]
                                     ).then_inc(sm["f5zd"].h, 16)
                blk.sync(sy_all)

    def _p5_h1(self, qt, sm, l2t, wu1t, h1_bf, lrow):
        nc = self.nc
        with ExitStack() as ps:
            ps_h = self.psum(ps, "p5h_%d" % qt, [128, 4096])
            w_done = {dc: sm["f5w%d" % (dc % 3)].inc(16) for dc in range(32)}
            hm_done = {dc: sm["f5hm"].inc() for dc in range(32)}
            with nc.Block() as blk:
                def g_all(g):
                    for dc in range(32):
                        sl = dc % 3
                        if dc >= 3:
                            g.wait_ge(sm["f5hm"].h, hm_done[dc - 3])
                        g.dma_start(wu1t[:, sl * 1024:(sl + 1) * 1024],
                                    self.Wu1T[dc * 128:(dc + 1) * 128, :]
                                    ).then_inc(sm["f5w%d" % sl].h, 16)
                blk.gpsimd(g_all)

                def pe_all(t):
                    for dc in range(32):
                        sl = dc % 3
                        t.wait_ge(sm["f5w%d" % sl].h, w_done[dc])
                        for m8 in range(8):
                            inst = t.matmul(
                                ps_h[:, m8 * 512:(m8 + 1) * 512],
                                wu1t[:, sl * 1024 + m8 * 128:
                                     sl * 1024 + (m8 + 1) * 128],
                                l2t[:, dc * 512:(dc + 1) * 512],
                                start=(dc == 0), stop=(dc == 31))
                        inst.then_inc(sm["f5hm"].h, 1)
                blk.tensor(pe_all)

                def s_all(s):
                    s.wait_ge(sm["f5hm"].h, hm_done[31])
                    for m8 in range(8):
                        s.activation(h1_bf[:, m8 * 512:(m8 + 1) * 512],
                                     ps_h[:, m8 * 512:(m8 + 1) * 512],
                                     AF.Gelu_apprx_tanh, scale=1.0,
                                     bias=self.bu1_sb[:, m8:m8 + 1]
                                     ).then_inc(sm["f5hb"].h, 1)
                    sm["f5hb"].inc(8)
                blk.scalar(s_all)

        with ExitStack() as ps:
            ps_l = self.psum(ps, "p5l_%d" % qt, [1, 512])
            with nc.Block() as blk:
                @blk.tensor
                def _(t):
                    for m8 in range(8):
                        inst = t.matmul(ps_l[:, :], self.Wu2T_sb[:, m8:m8 + 1],
                                        h1_bf[:, m8 * 512:(m8 + 1) * 512],
                                        start=(m8 == 0), stop=(m8 == 7))
                    inst.then_inc(sm["f5lm"].h, 1)

                @blk.vector
                def _(v):
                    v.wait_ge(sm["f5lm"].h, sm["f5lm"].inc(1))
                    v.tensor_copy(lrow[:, :], ps_l[:, :]).then_inc(sm["f5le"].h, 1)

                @blk.sync
                def _(sy):
                    sy.wait_ge(sm["f5le"].h, sm["f5le"].inc(1))
                    sy.dma_start(self.s_lr[0:1, qt * 512:(qt + 1) * 512],
                                 lrow[:, :]).then_inc(sm["f5ld"].h, 16)
                    sy.wait_ge(sm["f5ld"].h, sm["f5ld"].inc(16))

    def p6_collective(self):
        nc = self.nc
        sm = self.sems(["c6m", "c6e", "c6d", "c6c"])
        with ExitStack() as ph:
            srow = self.sbuf(ph, "c6row", [1, 16], F32)
            gs = self.sbuf(ph, "c6gs", [1, 4], F32)
            rgm = self.sbuf(ph, "c6rgm", [128, 1], F32)
            ps6 = self.psum(ph, "ps6", [1, 16], F32)
            with nc.Block() as blk:
                @blk.tensor
                def _(t):
                    t.matmul(ps6[:, :], self.onescolf[:, :], self.logv_sb[:, :],
                             start=True, stop=True).then_inc(sm["c6m"].h, 1)

                @blk.vector
                def _(v):
                    v.wait_ge(sm["c6m"].h, sm["c6m"].inc(1))
                    v.memset(gs[:, :], 0.0)
                    v.drain()
                    v.tensor_reduce(gs[:, 0:1], ps6[:, :], axis=AX.X,
                                    op=ALU.add).then_inc(sm["c6e"].h, 1)

                @blk.sync
                def _(sy):
                    sy.wait_ge(sm["c6e"].h, sm["c6e"].inc(1))
                    sy.dma_start(self.s_cc_in, gs[:, :]).then_inc(sm["c6d"].h, 16)
                    sy.wait_ge(sm["c6d"].h, sm["c6d"].inc(16))

            with nc.Block() as blk:
                @blk.gpsimd
                def _(g):
                    if _SIM_SINGLE:
                        g.dma_start(self.s_cc_out, self.s_cc_in
                                    ).then_inc(sm["c6c"].h, 16)
                        g.wait_ge(sm["c6c"].h, sm["c6c"].inc(16))
                    else:
                        g.collective_compute(
                            "AllReduce", mybir.AluOpType.add,
                            replica_groups=[[0, 1, 2, 3, 4, 5, 6, 7]],
                            ins=[self.s_cc_in.opt()],
                            outs=[self.s_cc_out.opt()],
                        ).then_inc(sm["c6c"].h, 1)
                        g.wait_ge(sm["c6c"].h, sm["c6c"].inc(1))

            with nc.Block() as blk:
                @blk.sync
                def _(sy):
                    sy.dma_start(gs[:, :], self.s_cc_out).then_inc(sm["c6d"].h, 16)
                    sy.wait_ge(sm["c6d"].h, sm["c6d"].inc(16))

            with nc.Block() as blk:
                @blk.vector
                def _(v):
                    v.tensor_scalar(gs[:, 0:1], gs[:, 0:1],
                                    self.cst[0:1, C16K:C16K + 1],
                                    self.cst[0:1, C1M6:C1M6 + 1],
                                    op0=ALU.mult, op1=ALU.add)
                    v.drain()
                    v.reciprocal(gs[:, 0:1], gs[:, 0:1])
                    v.drain()
                    v.tensor_scalar(gs[:, 0:1], gs[:, 0:1],
                                    self.cst[0:1, CHALF:CHALF + 1], None,
                                    op0=ALU.mult).then_inc(sm["c6e"].h, 1)

                @blk.sync
                def _(sy):
                    sy.wait_ge(sm["c6e"].h, sm["c6e"].inc(1))
                    sy.dma_start(self.s_row[2:3, 0:4], gs[:, :]
                                 ).then_inc(sm["c6d"].h, 16)
                    sy.wait_ge(sm["c6d"].h, sm["c6d"].inc(16))
                    sy.dma_start(rgm[:, :], self.s_row[2:3, 0:1]
                                 .to_broadcast([128, 1])).then_inc(sm["c6d"].h, 16)
                    sy.wait_ge(sm["c6d"].h, sm["c6d"].inc(16))

            with nc.Block() as blk:
                @blk.vector
                def _(v):
                    v.tensor_scalar(self.nvh_sb[:, :], self.logv_sb[:, :],
                                    rgm[:, :], None, op0=ALU.mult)

    def p7_final(self):
        nc = self.nc
        sm = self.sems(["o7x0", "o7x1", "o7z0", "o7z1", "o7l0", "o7l1", "o7g",
                        "o7c", "o7d0", "o7d1", "o7s"])
        with ExitStack() as ph:
            x2t = self.sbuf(ph, "o7x2", [128, 2 * D], F32)
            z2t = self.sbuf(ph, "o7z2", [128, 2 * D], BF16)
            lcol = self.sbuf(ph, "o7lc", [128, 1], F32)
            sig = self.sbuf(ph, "o7sig", [128, 1], F32)
            u_c = self.sbuf(ph, "o7u", [128, 1], F32)
            gate = self.sbuf(ph, "o7gate", [128, 1], F32)
            mex = self.sbuf(ph, "o7mex", [128, 1], F32)
            a1 = self.sbuf(ph, "o7a1", [128, 1], F32)
            a0 = self.sbuf(ph, "o7a0", [128, 1], F32)
            t1 = self.sbuf(ph, "o7t1", [128, 2 * D], F32)

            ops = {k: [] for k in "tsvyg"}
            x_done, z_done, l_done, g_done, c_done, d_done = {}, {}, {}, {}, {}, {}
            s_done = {}
            for mt in range(NMT):
                ws = mt % 2

                def xl(sy, mt=mt, ws=ws):
                    if mt >= 2:
                        sy.wait_ge(sm["o7c"].h, c_done[mt - 2])
                    sy.dma_start(x2t[:, ws * D:(ws + 1) * D],
                                 self.s_x2[mt * 128:(mt + 1) * 128, :]
                                 ).then_inc(sm["o7x%d" % ws].h, 16)
                    sy.dma_start(z2t[:, ws * D:(ws + 1) * D],
                                 self.s_z2[mt * 128:(mt + 1) * 128, :]
                                 ).then_inc(sm["o7z%d" % ws].h, 16)
                    sy.dma_start(lcol[:, :],
                                 self.s_lr[0:1, mt * 128:(mt + 1) * 128]
                                 .rearrange("a (p c) -> (a p) c", c=1),
                                 allow_slow_non_contiguous=True
                                 ).then_inc(sm["o7l%d" % ws].h, 16)
                ops["y"].append(xl)
                x_done[mt] = sm["o7x%d" % ws].inc(16)
                z_done[mt] = sm["o7z%d" % ws].inc(16)
                l_done[mt] = sm["o7l%d" % ws].inc(16)

                def gsig(s, mt=mt, ws=ws):
                    s.wait_ge(sm["o7l%d" % ws].h, l_done[mt])
                    if mt >= 1:
                        s.wait_ge(sm["o7g"].h, g_done[mt - 1])
                    s.activation(sig[:, :], lcol[:, :], AF.Sigmoid,
                                 scale=0.5, bias=self.bu2c[:, :]
                                 ).then_inc(sm["o7s"].h, 1)
                ops["s"].append(gsig)
                sg_v = sm["o7s"].inc()

                def gveg(v, mt=mt, sg_v=sg_v):
                    v.wait_ge(sm["o7s"].h, sg_v)
                    v.scalar_tensor_tensor(u_c[:, :], sig[:, :], 2.5,
                                           self.nvh_sb[:, mt:mt + 1],
                                           op0=ALU.mult, op1=ALU.add)
                    v.drain()
                    v.tensor_scalar(u_c[:, :], u_c[:, :],
                                    self.cst[:, C5:C5 + 1],
                                    self.cst[:, C0:C0 + 1],
                                    op0=ALU.min, op1=ALU.max)
                    v.drain()
                    v.tensor_scalar(u_c[:, :], u_c[:, :],
                                    self.cst[:, CHALF:CHALF + 1],
                                    self.cst[:, C1P5:C1P5 + 1],
                                    op0=ALU.subtract, op1=ALU.mult)
                    v.drain()
                    v.tensor_scalar(u_c[:, :], u_c[:, :],
                                    self.cst[:, C1:C1 + 1],
                                    self.cst[:, C0:C0 + 1],
                                    op0=ALU.min, op1=ALU.max)
                    v.drain()
                    v.scalar_tensor_tensor(u_c[:, :], u_c[:, :], self.gw1c[:, :],
                                           self.gbc[:, :],
                                           op0=ALU.mult, op1=ALU.add
                                           ).then_inc(sm["o7s"].h, 1)
                ops["v"].append(gveg)
                sg2_v = sm["o7s"].inc()

                def gsig2(s, mt=mt, sg2_v=sg2_v):
                    s.wait_ge(sm["o7s"].h, sg2_v)
                    s.activation(gate[:, :], u_c[:, :], AF.Sigmoid
                                 ).then_inc(sm["o7s"].h, 1)
                ops["s"].append(gsig2)
                sg3_v = sm["o7s"].inc()

                def gv2(v, mt=mt, ws=ws, sg3_v=sg3_v):
                    v.wait_ge(sm["o7s"].h, sg3_v)
                    v.tensor_scalar(mex[:, :], gate[:, :],
                                    self.cst[:, CEXIT:CEXIT + 1], None,
                                    op0=ALU.is_ge)
                    v.drain()
                    v.tensor_tensor(gate[:, :], gate[:, :], mex[:, :],
                                    op=ALU.mult)
                    v.drain()
                    v.tensor_scalar(a1[:, :], self.rstd2_sb[:, mt:mt + 1],
                                    self.cst[:, C1:C1 + 1], None, op0=ALU.add)
                    v.tensor_tensor(a0[:, :], self.mean2_sb[:, mt:mt + 1],
                                    self.rstd2_sb[:, mt:mt + 1], op=ALU.mult)
                    v.drain()
                    v.tensor_scalar(a0[:, :], a0[:, :],
                                    self.cst[:, CM1:CM1 + 1], None, op0=ALU.mult
                                    ).then_inc(sm["o7g"].h, 1)
                ops["v"].append(gv2)
                g_done[mt] = sm["o7g"].inc()

                def comb(v, mt=mt, ws=ws):
                    v.wait_ge(sm["o7g"].h, g_done[mt])
                    v.wait_ge(sm["o7x%d" % ws].h, x_done[mt])
                    v.wait_ge(sm["o7z%d" % ws].h, z_done[mt])
                    if mt >= 2:
                        v.wait_ge(sm["o7d%d" % ws].h, d_done[mt - 2])
                    v.tensor_scalar(t1[:, ws * D:(ws + 1) * D],
                                    x2t[:, ws * D:(ws + 1) * D],
                                    a1[:, :], a0[:, :],
                                    op0=ALU.mult, op1=ALU.add)
                    v.drain()
                    v.scalar_tensor_tensor(t1[:, ws * D:(ws + 1) * D],
                                           z2t[:, ws * D:(ws + 1) * D],
                                           gate[:, :],
                                           t1[:, ws * D:(ws + 1) * D],
                                           op0=ALU.mult, op1=ALU.add
                                           ).then_inc(sm["o7c"].h, 1)
                ops["v"].append(comb)
                c_done[mt] = sm["o7c"].inc()

                def odm(sy, mt=mt, ws=ws):
                    sy.wait_ge(sm["o7c"].h, c_done[mt])
                    sy.dma_start(self.out_own[mt * 128:(mt + 1) * 128, :],
                                 t1[:, ws * D:(ws + 1) * D]
                                 ).then_inc(sm["o7d%d" % ws].h, 16)
                ops["y"].append(odm)
                d_done[mt] = sm["o7d%d" % ws].inc(16)

            with nc.Block() as blk:
                blk.sync(lambda sy: [f(sy) for f in ops["y"]])
                blk.scalar(lambda s: [f(s) for f in ops["s"]])
                blk.vector(lambda v: [f(v) for f in ops["v"]])

            with nc.Block() as blk:
                f0, f1 = sm["o7d0"].n, sm["o7d1"].n

                @blk.sync
                def _(sy):
                    sy.wait_ge(sm["o7d0"].h, f0)
                    sy.wait_ge(sm["o7d1"].h, f1)

    def build_all(self):
        self.declare_io()
        sw_engines = [self.nc.tensor, self.nc.scalar, self.nc.vector,
                      self.nc.sync, self.nc.gpsimd]
        pid = self.nc.partition_id([e.engine for e in sw_engines])
        self.parity = pid % 2
        self.sw_engines = sw_engines
        with self.es:
            self.persistent_sbuf()
            self.p0_consts()
            self.p1a_ln1()
            self.p2_q()
            with ExitStack() as kv_stack:
                self.p1b_kv(kv_stack)
                self.p3_attention()
            self.p4_wo()
            self.p4c_ln2()
            self.p5_ffn()
            self.p6_collective()
            self.p7_final()
        return self.nc


def psq_slice(ps, k):
    return ps[:, k * 512:(k + 1) * 512]


def build():
    return Builder().build_all()


_NC_CACHE = {}


def _host_prep(inp):
    f32 = np.float32
    d = {}
    d["Wq"] = np.ascontiguousarray(
        np.asarray(inp["L0_attn_Wq"], f32).reshape(32, 128, 16, 256)
        .transpose(2, 0, 1, 3))
    d["Wk"] = np.ascontiguousarray(
        np.asarray(inp["L0_attn_Wk"], f32).reshape(32, 128, 512)
        .swapaxes(0, 1).reshape(128, 32 * 512))
    d["Wv"] = np.ascontiguousarray(
        np.asarray(inp["L0_attn_Wv"], f32).reshape(32, 128, 512)
        .swapaxes(0, 1).reshape(128, 32 * 512))
    d["Wo"] = np.ascontiguousarray(
        np.asarray(inp["L0_attn_Wo"], f32).reshape(32, 128, 4, 1024)
        .transpose(2, 1, 0, 3).reshape(4, 128, 32 * 1024))
    d["WqFT"] = np.ascontiguousarray(
        np.asarray(inp["L0_ffn_Wq"], f32).T.reshape(32, 128, 64)
        .swapaxes(0, 1).reshape(128, 32 * 64))
    d["WrT"] = np.ascontiguousarray(np.asarray(inp["L0_ffn_Wr"], f32).T)
    d["auxkT"] = np.ascontiguousarray(np.asarray(inp["L0_ffn_aux_keys"], f32).T)
    d["auxv"] = np.ascontiguousarray(
        np.asarray(inp["L0_ffn_aux_values"], f32).reshape(8, 128, 4096)
        .swapaxes(0, 1).reshape(128, 8 * 4096))
    d["WvdT"] = np.ascontiguousarray(
        np.asarray(inp["L0_ffn_Wvd"], f32).T.reshape(32, 128, 256)
        .swapaxes(0, 1).reshape(128, 32 * 256))
    d["WvuT"] = np.ascontiguousarray(
        np.asarray(inp["L0_ffn_Wvu"], f32).T.reshape(2, 128, 4096)
        .swapaxes(0, 1).reshape(128, 2 * 4096))
    d["Wu1T"] = np.ascontiguousarray(np.asarray(inp["L0_ffn_Wu1"], f32).T)
    d["Wu2T"] = np.ascontiguousarray(
        np.asarray(inp["L0_ffn_Wu2"], f32).reshape(1, 8, 128)[0].T)
    d["bu1c"] = np.ascontiguousarray(
        np.asarray(inp["L0_ffn_bu1"], f32).reshape(8, 128).T)
    d["bu2"] = np.asarray(inp["L0_ffn_bu2"], f32).reshape(1, 1)
    d["gw1"] = np.asarray(inp["L0_ffn_gate_w1"], f32).reshape(1, 1)
    d["gbias"] = np.asarray(inp["L0_ffn_gate_bias"], f32).reshape(1, 1)
    d["logrel"] = np.asarray(inp["L0_ffn_log_rel"], f32).reshape(1, NSLOT)
    d["ident"] = np.eye(128, dtype=f32)
    mp = np.zeros((128, 2048), f32)
    k = np.arange(128)[:, None]
    q = np.arange(512)[None, :]
    for j in range(2):
        for i in range(2):
            dd = (2 * j + i) * 128
            mp[:, j * 1024 + i * 512: j * 1024 + (i + 1) * 512] = (
                (k + dd) > q).astype(f32)
    d["maskp"] = mp
    return d


def kernel(**inp):
    x = np.asarray(inp["L0_attn_x"], np.float32)
    shared = _host_prep(inp)
    nc = build()
    in_maps = []
    for c in range(8):
        b, half = c >> 1, c & 1
        qt = QTILES[half]
        oq = QTILES[1 - half]
        xb = x[b].reshape(8, 512, D)
        m = dict(shared)
        m["x_own"] = np.ascontiguousarray(xb[qt].reshape(OWN, D))
        m["x_oth"] = np.ascontiguousarray(xb[oq].reshape(OWN, D))
        in_maps.append(m)
    res = run_bass_kernel_spmd(nc, in_maps, core_ids=list(range(8)))
    out = np.empty((B, S, D), np.float32)
    for c in range(8):
        b, half = c >> 1, c & 1
        qt = QTILES[half]
        oc = np.asarray(res.results[c]["out_own"]).reshape(4, 512, D)
        for j, t in enumerate(qt):
            out[b, t * 512:(t + 1) * 512] = oc[j]
    return out
